# version 12
# baseline (speedup 1.0000x reference)
import contextlib
import numpy as np
import ml_dtypes

import concourse.bass as bass
import concourse.mybir as mybir
from concourse.bass_utils import run_bass_kernel_spmd

F32 = mybir.dt.float32
BF16 = mybir.dt.bfloat16
AF = mybir.ActivationFunctionType
ALU = mybir.AluOpType

D = 2048
S = 4096
T = 2048
NH = 16
E = 128
HF = 8192
NG = 8
LIM = [1024, 2048, 3072, 4096]
NSLOT = 4
NMASK = 8
POS = {0: [0, 1536, 2048, 3584], 1: [512, 1024, 2560, 3072]}
EXP_BIAS = -30.0
INV_SQRT_DH = 0.08838834764831845
INV_SQRT_HF = 0.011048543456039806


class SemT:

    def __init__(self, ctx, nc, name):
        self.h = ctx.enter_context(nc.semaphore(name))
        self.n = 0

    is_dma = False

    def dma(self):
        self.is_dma = True
        self.n += 16
        return self.n

    def bump(self):
        self.n += 1
        return self.n


class Q:

    def __init__(self):
        self.pe = []
        self.act = []
        self.dve = []
        self.pool = []
        self.sp = []

    def run_block(self, nc, name):
        with nc.Block(name):
            blk = None


def retire_dmas(q, sems, last):
    for nm, s_ in sems.items():
        if s_.is_dma and s_.n > last.get(nm, 0):
            W(q.act, s_, s_.n)
            last[nm] = s_.n


def emit_block(nc, q, name):
    with nc.Block(name) as blk:
        @blk.tensor
        def _(e):
            for f in q.pe:
                f(e)

        @blk.scalar
        def _(e):
            for f in q.act:
                f(e)

        @blk.vector
        def _(e):
            for f in q.dve:
                f(e)

        @blk.gpsimd
        def _(e):
            for f in q.pool:
                f(e)

        @blk.sync
        def _(e):
            for f in q.sp:
                f(e)


def W(lst, sem, val):
    if val > 0:
        lst.append(lambda e, h=sem.h, v=val: e.wait_ge(h, v))


def DI(lst, sem):
    v = sem.bump()
    lst.append(lambda e, h=sem.h: e.drain().then_inc(h, 1))
    return v


def DR(lst):
    lst.append(lambda e: e.drain())


def build(dump=None, phases=("p0", "att", "wo", "fa", "fb")):
    nc = bass.Bass("TRN2", target_bir_lowering=False, debug=False)

    def dt_kind(name):
        return "ExternalOutput" if dump == name else "Internal"

    x_own = nc.dram_tensor("x_own", [T, D], F32, kind="ExternalInput").ap()
    x_batch = nc.dram_tensor("x_batch", [S, D], F32, kind="ExternalInput").ap()
    wq = nc.dram_tensor("wq", [NH, D, E], F32, kind="ExternalInput").ap()
    wk = nc.dram_tensor("wk", [NH, D, E], F32, kind="ExternalInput").ap()
    wvr = nc.dram_tensor("wvr", [D, NH * E], F32, kind="ExternalInput").ap()
    wo = nc.dram_tensor("wo", [NH, E, D], F32, kind="ExternalInput").ap()
    w1t = nc.dram_tensor("w1t", [D, HF], F32, kind="ExternalInput").ap()
    w3t = nc.dram_tensor("w3t", [D, HF], F32, kind="ExternalInput").ap()
    w2t = nc.dram_tensor("w2t", [HF, D], F32, kind="ExternalInput").ap()
    bqt = nc.dram_tensor("bqt", [E, NH], F32, kind="ExternalInput").ap()
    bkt = nc.dram_tensor("bkt", [E, NH], F32, kind="ExternalInput").ap()
    bvb = nc.dram_tensor("bvb", [128, NH * E], F32, kind="ExternalInput").ap()
    bob = nc.dram_tensor("bob", [128, D], F32, kind="ExternalInput").ap()
    qmp = nc.dram_tensor("qmp", [NSLOT, 128, 512], F32, kind="ExternalInput").ap()
    h64 = nc.dram_tensor("h64", [64, 64], F32, kind="ExternalInput").ap()
    h128 = nc.dram_tensor("h128", [128, 128], F32, kind="ExternalInput").ap()

    out = nc.dram_tensor("out", [T, D], F32, kind="ExternalOutput").ap()

    xob = nc.dram_tensor("xob", [T, D], BF16, kind=dt_kind("xob")).ap()
    xof = nc.dram_tensor("xof", [D, T], BF16, kind=dt_kind("xof")).ap()
    xbb = nc.dram_tensor("xbb", [S, D], BF16, kind=dt_kind("xbb")).ap()
    xbf = nc.dram_tensor("xbf", [D, S], BF16, kind=dt_kind("xbf")).ap()
    w2p = nc.dram_tensor("w2p", [HF, D], BF16, kind=dt_kind("w2p")).ap()
    w1b16 = nc.dram_tensor("w1b16", [D, HF], BF16, kind="Internal").ap()
    w3b16 = nc.dram_tensor("w3b16", [D, HF], BF16, kind="Internal").ap()
    zf = nc.dram_tensor("zf", [NH * E, T], BF16, kind=dt_kind("zf")).ap()
    yb = nc.dram_tensor("yb", [T, D], BF16, kind=dt_kind("yb")).ap()
    y32 = nc.dram_tensor("y32", [T, D], F32, kind=dt_kind("y32")).ap()
    hh = nc.dram_tensor("hh", [HF, T], BF16, kind=dt_kind("hh")).ap()

    with contextlib.ExitStack() as ctx:
        E_ = ctx.enter_context
        psum = E_(nc.psum_tensor("psum", [128, 4096], F32))
        pba = psum.ap()
        pb = [pba[:, i * 512:(i + 1) * 512] for i in range(8)]
        h64b = E_(nc.sbuf_tensor("h64b", [64, 64], BF16))
        h128b = E_(nc.sbuf_tensor("h128b", [128, 128], BF16))
        expb = E_(nc.sbuf_tensor("expb", [128, 1], F32))
        scl128 = E_(nc.sbuf_tensor("scl128", [128, 1], F32))
        sc8192 = E_(nc.sbuf_tensor("sc8192", [128, 1], F32))
        ones_g = E_(nc.sbuf_tensor("ones_g", [1, 128], BF16))

        sems = {}
        retired = {}

        def SEM(name):
            if name not in sems:
                sems[name] = SemT(ctx, nc, name)
            return sems[name]

        def DMA(lst, sem_name, fn, wait_tok=None):
            s = SEM(sem_name)
            if wait_tok is not None:
                W(lst, *wait_tok)
            v = s.dma()
            lst.append(lambda e, fn=fn, h=s.h: fn(e).then_inc(h, 16))
            return (s, v)

        if "p0" in phases:
            with contextlib.ExitStack() as p0ctx:
                P = p0ctx.enter_context
                xo_sb = [P(nc.sbuf_tensor(f"xo_sb{i}", [128, 2 * D], BF16)) for i in range(4)]
                xf_sb = [P(nc.sbuf_tensor(f"xf_sb{i}", [128, 2 * D], F32)) for i in range(4)]
                tr_sb = [P(nc.sbuf_tensor(f"tr_sb{i}", [128, S], BF16, is_transpose_dest=True)) for i in range(4)]
                w2i = [P(nc.sbuf_tensor(f"w2i{i}", [64, D], BF16)) for i in range(4)]
                w2o = [P(nc.sbuf_tensor(f"w2o{i}", [64, D], BF16)) for i in range(4)]

                q = Q()
                t_h64 = DMA(q.pool, "s_cst0", lambda e: e.dma_start(out=h64b[:], in_=h64))
                t_h128 = DMA(q.pool, "s_cst1", lambda e: e.dma_start(out=h128b[:], in_=h128))
                q.dve.append(lambda e: e.memset(expb[:], EXP_BIAS))
                q.dve.append(lambda e: e.memset(scl128[:], INV_SQRT_DH))
                q.dve.append(lambda e: e.memset(sc8192[:], INV_SQRT_HF))
                q.dve.append(lambda e: e.memset(ones_g[:], 1.0))

                st_tok = [None] * 4
                cv_tok = [None] * 4
                own_tok = [None] * 4
                for tt in range(T // 256 + S // 256):
                    par = tt % 4
                    own = tt < T // 256
                    off = tt if own else tt - T // 256
                    srcm = x_own if own else x_batch
                    dstm = xob if own else xbb
                    lt = DMA(q.sp, f"s_xoc{par}",
                             lambda e, off=off, par=par, srcm=srcm: e.dma_start(
                                 out=xf_sb[par].ap().rearrange("p (b d) -> p b d", b=2),
                                 in_=srcm[off * 256:(off + 1) * 256, :].rearrange("(b p) d -> p b d", p=128)),
                             wait_tok=cv_tok[par])
                    W(q.dve, *lt)
                    if st_tok[par] is not None:
                        W(q.dve, *st_tok[par])
                    q.dve.append(lambda e, par=par: e.tensor_copy(xo_sb[par][:], xf_sb[par][:]))
                    cv = DI(q.dve, SEM("s_xcv"))
                    cv_tok[par] = (SEM("s_xcv"), cv)
                    W(q.act, SEM("s_xcv"), cv)
                    st_tok[par] = DMA(q.act, f"s_xow{par}",
                                      lambda e, off=off, par=par, dstm=dstm: e.dma_start(
                                          out=dstm[off * 256:(off + 1) * 256, :].rearrange("(b p) d -> p b d", p=128),
                                          in_=xo_sb[par].ap().rearrange("p (b d) -> p b d", b=2)))
                    if own:
                        own_tok[par] = st_tok[par]
                xow_all = list(st_tok)
                tr_tok = [None] * 4
                for j in range(D // 128):
                    par = j % 4
                    for tk in own_tok:
                        W(q.sp, *tk)
                    tt2 = DMA(q.sp, f"s_xot{par}",
                              lambda e, j=j, par=par: e.dma_start_transpose(
                                  out=tr_sb[par][:, 0:T], in_=xob[:, j * 128:(j + 1) * 128]),
                              wait_tok=tr_tok[par])
                    W(q.act, *tt2)
                    tr_tok[par] = DMA(q.act, f"s_xfw{par}",
                                      lambda e, j=j, par=par: e.dma_start(
                                          out=xof[j * 128:(j + 1) * 128, :], in_=tr_sb[par][:, 0:T]))
                xfw_all = list(tr_tok)
                for j in range(D // 128):
                    par = j % 4
                    for tk in xow_all:
                        W(q.sp, *tk)
                    tt2 = DMA(q.sp, f"s_xot{par}",
                              lambda e, j=j, par=par: e.dma_start_transpose(
                                  out=tr_sb[par][:, 0:S], in_=xbb[:, j * 128:(j + 1) * 128]),
                              wait_tok=tr_tok[par])
                    W(q.act, *tt2)
                    tr_tok[par] = DMA(q.act, f"s_xfw{par}",
                                      lambda e, j=j, par=par: e.dma_start(
                                          out=xbf[j * 128:(j + 1) * 128, :], in_=tr_sb[par][:, 0:S]))
                xbfw_all = list(tr_tok)

                w2t_v = w2t.rearrange("(j q) d -> q j d", q=128)
                w2p_v = w2p.rearrange("(j q) d -> q j d", q=128)
                W(q.pe, *t_h64)
                wr_tok = [None] * 4
                mm_tok = [None] * 4
                fbank_tok = [None] * 4
                for p in range(128):
                    fpar = p % 4
                    lt = DMA(q.pool, f"s_w2l{fpar}",
                             lambda e, p=p, fpar=fpar: e.dma_start(out=w2i[fpar][:], in_=w2t_v[p]),
                             wait_tok=mm_tok[fpar])
                    W(q.pe, *lt)
                    if wr_tok[fpar] is not None:
                        W(q.dve, *wr_tok[fpar])
                    for c in range(4):
                        if fbank_tok[c] is not None:
                            W(q.pe, *fbank_tok[c])
                    for c in range(4):
                        q.pe.append(lambda e, fpar=fpar, c=c: e.matmul(
                            pb[4 + c][0:64, :], h64b[:], w2i[fpar][:, c * 512:(c + 1) * 512],
                            start=True, stop=True))
                    mv = DI(q.pe, SEM("s_w2mm"))
                    mm_tok[fpar] = (SEM("s_w2mm"), mv)
                    W(q.dve, SEM("s_w2mm"), mv)
                    for c in range(4):
                        q.dve.append(lambda e, fpar=fpar, c=c: e.tensor_copy(
                            w2o[fpar][:, c * 512:(c + 1) * 512], pb[4 + c][0:64, :]))
                    ev = DI(q.dve, SEM("s_w2ev"))
                    for c in range(4):
                        fbank_tok[c] = (SEM("s_w2ev"), ev)
                    W(q.act, SEM("s_w2ev"), ev)
                    wr_tok[fpar] = DMA(q.act, f"s_w2w{fpar}",
                                       lambda e, p=p, fpar=fpar: e.dma_start(out=w2p_v[p], in_=w2o[fpar][:]))

                retire_dmas(q, sems, retired)
                emit_block(nc, q, "p0")
        else:
            xfw_all = xbfw_all = [None]

        if "att" in phases:
            with contextlib.ExitStack() as actx:
                A = actx.enter_context
                K2 = A(nc.sbuf_tensor("K2", [128, 2 * S], BF16))
                V2e = A(nc.sbuf_tensor("V2e", [128, 32 * 258], BF16))
                Q2 = A(nc.sbuf_tensor("Q2", [128, 2 * T], BF16))
                PT = [A(nc.sbuf_tensor("PT0", [128, 24 * 512], BF16)),
                      A(nc.sbuf_tensor("PT1", [128, 32 * 512], BF16))]
                WQb = [A(nc.sbuf_tensor(f"WQb{i}", [128, 4096], BF16)) for i in range(2)]
                WKb = [A(nc.sbuf_tensor(f"WKb{i}", [128, 4096], BF16)) for i in range(2)]
                WVb = [A(nc.sbuf_tensor(f"WVb{i}", [128, 4096], BF16)) for i in range(2)]
                xcol = [A(nc.sbuf_tensor(f"xcol{i}", [128, 8192], BF16)) for i in range(2)]
                zTst = [A(nc.sbuf_tensor(f"zTst{i}", [128, T], BF16, is_transpose_dest=True)) for i in range(2)]
                rd_bf = [A(nc.sbuf_tensor(f"rd_bf{i}", [1, 512], BF16)) for i in range(2)]
                rd_f32 = [A(nc.sbuf_tensor(f"rd_f32{i}", [1, 512], F32)) for i in range(2)]
                rb_sb = [A(nc.sbuf_tensor("rb_sb0", [128, 512], F32))] * 2
                qmp_s = A(nc.sbuf_tensor("qmp_s", [128, NSLOT * 512], F32))
                bqt_s = A(nc.sbuf_tensor("bqt_s", [128, NH], F32))
                bkt_s = A(nc.sbuf_tensor("bkt_s", [128, NH], F32))
                bvb_g = [A(nc.sbuf_tensor(f"bvb_g{i}", [128, 256], F32)) for i in range(2)]

                q = Q()
                DMA(q.act, "s_catt", lambda e: e.dma_start(
                    out=qmp_s.ap().rearrange("p (s f) -> p s f", s=NSLOT),
                    in_=qmp.rearrange("s p f -> p s f")))
                DMA(q.act, "s_catt", lambda e: e.dma_start(out=bqt_s[:], in_=bqt))
                ct = [DMA(q.act, "s_catt", lambda e: e.dma_start(out=bkt_s[:], in_=bkt))]

                w13s = [A(nc.sbuf_tensor(f"w13s{i}", [128, 2048], BF16)) for i in range(1)]
                v2e_ones = V2e.ap().rearrange("p (k h e) -> p k h e", k=32, h=2)[:, :, :, 128:129]
                q.pool.append(lambda e: e.memset(v2e_ones, 1.0))
                ones_tok = (SEM("s_misc"), DI(q.pool, SEM("s_misc")))

                xof_v = xof.rearrange("(c p) t -> p c t", p=128)
                xbf_v = xbf.rearrange("(c p) t -> p c t", p=128)
                xcol_v = [xcol[i].ap().rearrange("p (c t) -> p c t", c=16) for i in range(2)]

                projdone = [None]

                bvb_free = [None, None]

                def load_group_weights(g, prev_projdone):
                    par = g % 2
                    toks = []
                    toks.append(DMA(q.pool, f"s_bv{par}",
                                    lambda e, g=g, par=par: e.dma_start(
                                        out=bvb_g[par][:], in_=bvb[:, g * 256:(g + 1) * 256]),
                                    wait_tok=bvb_free[par]))
                    wq_src = wq[2 * g:2 * g + 2].rearrange("h (c p) e -> p h c e", p=128)
                    wk_src = wk[2 * g:2 * g + 2].rearrange("h (c p) e -> p h c e", p=128)
                    wv_src = wvr[:, g * 256:(g + 1) * 256].rearrange("(c p) n -> p c n", p=128)
                    toks.append(DMA(q.pool, f"s_wq{par}", lambda e, s=wq_src, d=WQb[par]: e.dma_start(
                        out=d.ap().rearrange("p (h c e) -> p h c e", h=2, c=16), in_=s),
                        wait_tok=prev_projdone))
                    toks.append(DMA(q.pool, f"s_wk{par}", lambda e, s=wk_src, d=WKb[par]: e.dma_start(
                        out=d.ap().rearrange("p (h c e) -> p h c e", h=2, c=16), in_=s)))
                    toks.append(DMA(q.pool, f"s_wv{par}", lambda e, s=wv_src, d=WVb[par]: e.dma_start(
                        out=d.ap().rearrange("p (c n) -> p c n", c=16), in_=s)))
                    return toks

                def load_xcols(g):
                    toks = []
                    for i in range(12):
                        gi = g * 12 + i
                        bi = gi % 2
                        wait_tok = None
                        if i < 2:
                            if g >= 1:
                                wait_tok = (SEM("s_pk"), 8 * (g - 1) + (7 + i))
                        elif i < 4:
                            wait_tok = (SEM("s_pq"), 2 * g + 1)
                        elif i < 6:
                            wait_tok = (SEM("s_pq"), 2 * g + 2)
                        else:
                            wait_tok = (SEM("s_pk"), 8 * g + (i - 6) + 1)
                        if i < 4:
                            src = xof_v[:, :, i * 512:(i + 1) * 512]
                        else:
                            kc = i - 4
                            src = xbf_v[:, :, kc * 512:(kc + 1) * 512]
                        toks.append(DMA(q.pool, f"s_xl{bi}",
                                        lambda e, s=src, d=xcol_v[bi]: e.dma_start(out=d, in_=s),
                                        wait_tok=wait_tok))
                    return toks

                if xfw_all[0] is not None:
                    for tk in xfw_all + xbfw_all:
                        if tk is not None:
                            W(q.pool, *tk)

                w13_src = [w1t.rearrange("(c p) f -> p c f", p=128), w3t.rearrange("(c p) f -> p c f", p=128)]
                w13_dst = [w1b16.rearrange("(c p) f -> p c f", p=128), w3b16.rearrange("(c p) f -> p c f", p=128)]
                w13_st = [None, None]

                def conv_w13(k):
                    wsel, r = divmod(k, 64)
                    cidx, fidx = r % 8, r // 8
                    par = 0
                    lt = DMA(q.pool, f"s_wcv{par}",
                             lambda e, wsel=wsel, cidx=cidx, fidx=fidx, par=par: e.dma_start(
                                 out=w13s[par].ap().rearrange("p (c f) -> p c f", c=2),
                                 in_=w13_src[wsel][:, 2 * cidx:2 * cidx + 2, 1024 * fidx:1024 * fidx + 1024]),
                             wait_tok=w13_st[par])
                    W(q.sp, *lt)
                    w13_st[par] = DMA(q.sp, f"s_wcw{par}",
                                      lambda e, wsel=wsel, cidx=cidx, fidx=fidx, par=par: e.dma_start(
                                          out=w13_dst[wsel][:, 2 * cidx:2 * cidx + 2, 1024 * fidx:1024 * fidx + 1024],
                                          in_=w13s[par].ap().rearrange("p (c f) -> p c f", c=2)))

                wg_toks = load_group_weights(0, None)
                x_toks = load_xcols(0)

                tokA = [None] * 4
                tokZ = [None] * 4
                zhead = 0
                ztr_tok = [None] * 4
                zfw_tok = [None, None]
                for t in ct:
                    W(q.pe, *t)
                    W(q.act, *t)
                    W(q.dve, *t)
                W(q.pe, *ones_tok)

                for g in range(NG):
                    par = g % 2
                    for k in range(16 * g, 16 * g + 16):
                        conv_w13(k)
                    for t in wg_toks:
                        W(q.pe, *t)

                    for tqp in range(2):
                        for tq in (2 * tqp, 2 * tqp + 1):
                            gi = g * 12 + tq
                            bi = gi % 2
                            W(q.pe, *x_toks[tq])
                            for h in range(2):
                                bank = (tq % 2) * 2 + h
                                if tokA[bank] is not None:
                                    W(q.pe, *tokA[bank])
                                for c in range(16):
                                    q.pe.append(lambda e, par=par, h=h, c=c, bi=bi, tq=tq, bank=bank: e.matmul(
                                        pb[bank][:], WQb[par][:, (h * 16 + c) * 128:(h * 16 + c + 1) * 128],
                                        xcol[bi][:, c * 512:(c + 1) * 512],
                                        start=(c == 0), stop=(c == 15)))
                        mv = DI(q.pe, SEM("s_pq"))
                        W(q.dve, SEM("s_pq"), mv)
                        for tq in (2 * tqp, 2 * tqp + 1):
                            for h in range(2):
                                hg = 2 * g + h
                                bank = (tq % 2) * 2 + h
                                q.dve.append(lambda e, h=h, tq=tq, hg=hg, bank=bank: e.tensor_scalar(
                                    out=Q2[:, h * T + tq * 512: h * T + (tq + 1) * 512],
                                    in0=pb[bank][:], scalar1=bqt_s[:, hg:hg + 1], scalar2=None,
                                    op0=ALU.add))
                        ev = DI(q.dve, SEM("s_pqf"))
                        for b in range(4):
                            tokA[b] = (SEM("s_pqf"), ev)

                    for kcp in range(4):
                        for kc in (2 * kcp, 2 * kcp + 1):
                            gi = g * 12 + 4 + kc
                            bi = gi % 2
                            W(q.pe, *x_toks[4 + kc])
                            for h in range(2):
                                bank = (kc % 2) * 2 + h
                                if tokA[bank] is not None:
                                    W(q.pe, *tokA[bank])
                                for c in range(16):
                                    q.pe.append(lambda e, par=par, h=h, c=c, bi=bi, bank=bank: e.matmul(
                                        pb[bank][:], WKb[par][:, (h * 16 + c) * 128:(h * 16 + c + 1) * 128],
                                        xcol[bi][:, c * 512:(c + 1) * 512],
                                        start=(c == 0), stop=(c == 15)))
                            for kt in range(4):
                                ktile = kc * 4 + kt
                                zbank = 4 + kt
                                if tokZ[kt] is not None:
                                    W(q.pe, *tokZ[kt])
                                for c in range(16):
                                    q.pe.append(lambda e, c=c, bi=bi, kt=kt, par=par, zbank=zbank: e.matmul(
                                        pb[zbank][:, 0:256],
                                        xcol[bi][:, c * 512 + kt * 128: c * 512 + (kt + 1) * 128],
                                        WVb[par][:, c * 256:(c + 1) * 256],
                                        start=(c == 0), stop=(c == 15)))
                            mv = DI(q.pe, SEM("s_pk"))
                            W(q.dve, SEM("s_pk"), mv)
                            for h in range(2):
                                hg = 2 * g + h
                                bank = (kc % 2) * 2 + h
                                q.dve.append(lambda e, h=h, kc=kc, hg=hg, bank=bank: e.tensor_scalar(
                                    out=K2[:, h * S + kc * 512: h * S + (kc + 1) * 512],
                                    in0=pb[bank][:], scalar1=bkt_s[:, hg:hg + 1], scalar2=None,
                                    op0=ALU.add))
                            kev = DI(q.dve, SEM("s_pkf"))
                            tokA[(kc % 2) * 2] = (SEM("s_pkf"), kev)
                            tokA[(kc % 2) * 2 + 1] = (SEM("s_pkf"), kev)
                            for kt in range(4):
                                ktile = kc * 4 + kt
                                vout = V2e.ap().rearrange("p (k h e) -> p k h e", k=32, h=2)[:, ktile, :, 0:128]
                                q.dve.append(lambda e, vout=vout, kt=kt, par=par: e.tensor_tensor(
                                    out=vout,
                                    in0=pb[4 + kt][:, 0:256].rearrange("p (h e) -> p h e", h=2),
                                    in1=bvb_g[par].ap().rearrange("p (h e) -> p h e", h=2),
                                    op=ALU.add))
                            vev = DI(q.dve, SEM("s_pvf"))
                            bvb_free[par] = (SEM("s_pvf"), vev)
                            for kt in range(4):
                                tokZ[kt] = (SEM("s_pvf"), vev)
                    projdone[0] = (SEM("s_pk"), 8 * (g + 1))

                    if g + 1 < NG:
                        wg_next = load_group_weights(g + 1, projdone[0])
                        x_next = load_xcols(g + 1)

                    pkf_target = SEM("s_pkf").n
                    pqf_target = SEM("s_pqf").n
                    pvf_target = SEM("s_pvf").n

                    for h in range(2):
                        hg = 2 * g + h
                        W(q.pe, SEM("s_pkf"), pkf_target)
                        W(q.pe, SEM("s_pqf"), pqf_target)
                        W(q.pe, SEM("s_pvf"), pvf_target)
                        if zfw_tok[hg % 2] is not None:
                            W(q.dve, *zfw_tok[hg % 2])

                        slot_state = []

                        def emit_scores(s):
                            nj = LIM[s] // 128
                            exp_base = SEM("s_exp").n
                            for j in range(nj):
                                bank = j % 4
                                if j < 4:
                                    if tokA[bank] is not None:
                                        W(q.pe, *tokA[bank])
                                else:
                                    W(q.pe, SEM("s_exp"), exp_base + j // 4)
                                q.pe.append(lambda e, h=h, j=j, s=s, bank=bank: e.matmul(
                                    pb[bank][:], K2[:, h * S + j * 128:h * S + (j + 1) * 128],
                                    Q2[:, h * T + s * 512:h * T + (s + 1) * 512],
                                    start=True, stop=True))
                                if j % 4 == 3:
                                    mv = DI(q.pe, SEM("s_sc"))
                                    W(q.act, SEM("s_sc"), mv)
                                    W(q.act, SEM("s_av"), SEM("s_av").n)
                                    jj0 = j - 3
                                    q.act.append(lambda e, jj0=jj0, sp2=s % 2: e.activation(
                                        out=PT[sp2][:, jj0 * 512:(jj0 + 4) * 512], in_=pba[:, 0:2048],
                                        func=AF.Exp, bias=expb[:], scale=scl128[:]))
                                    DI(q.act, SEM("s_exp"))
                            exp_total = SEM("s_exp").n
                            for b in range(4):
                                tokA[b] = (SEM("s_exp"), exp_total)
                            exp_base2 = exp_base
                            for j in range(nj - NMASK, nj):
                                W(q.dve, SEM("s_exp"), exp_base2 + j // 4 + 1)
                                q.dve.append(lambda e, j=j, s=s, sp2=s % 2: e.scalar_tensor_tensor(
                                    out=PT[sp2][:, j * 512:(j + 1) * 512],
                                    in0=qmp_s[:, s * 512:(s + 1) * 512], scalar=float(128 * j),
                                    in1=PT[sp2][:, j * 512:(j + 1) * 512],
                                    op0=ALU.is_ge, op1=ALU.mult))
                            mk = DI(q.dve, SEM("s_mask"))
                            return (s, exp_total, mk, nj)

                        pending = [None]

                        def emit_norm():
                            if pending[0] is None:
                                return
                            s_, rc = pending[0]
                            pending[0] = None
                            pz = s_ % 2
                            Ab, Db = 4 + 2 * pz, 5 + 2 * pz
                            W(q.pe, SEM("s_zev"), rc)
                            q.pe.append(lambda e, Db=Db, pz=pz: e.matmul(
                                pb[Db][:], ones_g[:], rd_bf[pz][:], start=True, stop=True))
                            bc = DI(q.pe, SEM("s_av"))
                            W(q.dve, SEM("s_av"), bc)
                            q.dve.append(lambda e, Db=Db, pz=pz: e.tensor_copy(rb_sb[pz][:], pb[Db][:]))
                            DR(q.dve)
                            q.dve.append(lambda e, Ab=Ab, pz=pz, s_=s_, h=h: e.tensor_tensor(
                                out=zTst[h][:, s_ * 512:(s_ + 1) * 512], in0=pb[Ab][:],
                                in1=rb_sb[pz][:], op=ALU.mult))
                            nv = DI(q.dve, SEM("s_zev"))
                            tokZ[2 * pz] = (SEM("s_zev"), nv)
                            tokZ[1 + 2 * pz] = (SEM("s_zev"), nv)

                        def emit_av(state):
                            s, exp_total, mk, nj = state
                            pz = s % 2
                            Ab, Db = 4 + 2 * pz, 5 + 2 * pz
                            jorder = list(range(nj - NMASK)) + list(range(nj - NMASK, nj))
                            W(q.pe, SEM("s_exp"), exp_total)
                            if tokZ[2 * pz] is not None:
                                W(q.pe, *tokZ[2 * pz])
                            if tokZ[1 + 2 * pz] is not None:
                                W(q.pe, *tokZ[1 + 2 * pz])
                            for ji, j in enumerate(jorder):
                                if j == nj - NMASK or (nj == NMASK and ji == 0):
                                    W(q.pe, SEM("s_mask"), mk)
                                q.pe.append(lambda e, j=j, h=h, Ab=Ab, ji=ji, nj=nj, sp2=s % 2: e.matmul(
                                    pb[Ab][:], V2e[:, j * 258 + h * 129: j * 258 + h * 129 + 128],
                                    PT[sp2][:, j * 512:(j + 1) * 512],
                                    start=(ji == 0), stop=(ji == nj - 1)))
                            for ji, j in enumerate(jorder):
                                q.pe.append(lambda e, j=j, h=h, Db=Db, ji=ji, nj=nj, sp2=s % 2: e.matmul(
                                    pb[Db][0:1, :], V2e[:, j * 258 + h * 129 + 128: j * 258 + h * 129 + 129],
                                    PT[sp2][:, j * 512:(j + 1) * 512],
                                    start=(ji == 0), stop=(ji == nj - 1)))
                            av = DI(q.pe, SEM("s_av"))
                            W(q.dve, SEM("s_av"), av)
                            q.dve.append(lambda e, Db=Db, pz=pz: e.reciprocal(
                                rd_f32[pz][:], pb[Db][0:1, 0:512]))
                            DR(q.dve)
                            q.dve.append(lambda e, pz=pz: e.tensor_copy(rd_bf[pz][:], rd_f32[pz][:]))
                            rc = DI(q.dve, SEM("s_zev"))
                            pending[0] = (s, rc)

                        prev = None
                        for s in range(NSLOT):
                            st = emit_scores(s)
                            emit_norm()
                            if prev is not None:
                                emit_av(prev)
                            prev = st
                        emit_norm()
                        emit_av(prev)
                        emit_norm()

                        zl_final = SEM("s_zev").n
                        W(q.sp, SEM("s_zev"), zl_final)
                        zfw_tok[hg % 2] = DMA(q.sp, f"s_zfw{hg % 2}",
                                              lambda e, hg=hg, h=h: e.dma_start(
                                                  out=zf[hg * 128:(hg + 1) * 128, :], in_=zTst[h][:]))
                    if g + 1 < NG:
                        wg_toks = wg_next
                        x_toks = x_next

                zfw_final = [zfw_tok[0], zfw_tok[1]]
                retire_dmas(q, sems, retired)
                emit_block(nc, q, "att")
        else:
            zfw_final = [None, None]

        if "wo" in phases:
            with contextlib.ExitStack() as wctx:
                A = wctx.enter_context
                z_fm = A(nc.sbuf_tensor("z_fm", [128, NH * T], BF16))
                WOb = A(nc.sbuf_tensor("WOb", [128, NH * D], BF16))
                bob_s = A(nc.sbuf_tensor("bob_s", [128, D], F32))
                xo_t = [A(nc.sbuf_tensor(f"xo_t{i}", [128, D], F32)) for i in range(2)]
                row = [A(nc.sbuf_tensor(f"row{i}", [128, D], F32)) for i in range(2)]
                y32t = [A(nc.sbuf_tensor(f"y32t{i}", [128, D], F32)) for i in range(2)]
                ybt = [A(nc.sbuf_tensor(f"ybt{i}", [128, D], BF16)) for i in range(2)]
                rsum = [A(nc.sbuf_tensor(f"rsum{i}", [128, 1], F32)) for i in range(2)]
                nmean = [A(nc.sbuf_tensor(f"nmean{i}", [128, 1], F32)) for i in range(2)]
                varp = [A(nc.sbuf_tensor(f"varp{i}", [128, 1], F32)) for i in range(2)]
                stdv = [A(nc.sbuf_tensor(f"stdv{i}", [128, 1], F32)) for i in range(2)]
                rstd = [A(nc.sbuf_tensor(f"rstd{i}", [128, 1], F32)) for i in range(2)]

                q = Q()
                if zfw_final[0] is not None:
                    W(q.sp, *zfw_final[0])
                    W(q.sp, *zfw_final[1])
                t_z = DMA(q.sp, "s_zl", lambda e: e.dma_start(
                    out=z_fm.ap().rearrange("p (h t) -> p h t", h=NH),
                    in_=zf.rearrange("(h p) t -> p h t", p=128)))
                t_wo = DMA(q.pool, "s_wol", lambda e: e.dma_start(
                    out=WOb.ap().rearrange("p (h d) -> p h d", h=NH),
                    in_=wo.rearrange("h p d -> p h d")))
                t_bo = DMA(q.act, "s_cbo", lambda e: e.dma_start(out=bob_s[:], in_=bob))
                W(q.pe, *t_z)
                W(q.pe, *t_wo)
                W(q.dve, *t_bo)

                ln_tok = [None, None]
                st_tok = [None, None]
                ybt_tok = [None]
                row_tok = None
                for tt in range(16):
                    par = tt % 2
                    if ln_tok[par] is not None:
                        W(q.act, *ln_tok[par])
                    t_x = DMA(q.act, f"s_xrl{par}",
                              lambda e, tt=tt, par=par: e.dma_start(
                                  out=xo_t[par][:], in_=x_own[tt * 128:(tt + 1) * 128, :]))
                    if row_tok is not None:
                        W(q.pe, *row_tok)
                    for dc in range(4):
                        for h in range(NH):
                            q.pe.append(lambda e, h=h, tt=tt, dc=dc: e.matmul(
                                pb[dc][:], z_fm[:, h * T + tt * 128:h * T + (tt + 1) * 128],
                                WOb[:, h * D + dc * 512:h * D + (dc + 1) * 512],
                                start=(h == 0), stop=(h == NH - 1)))
                    mv = DI(q.pe, SEM("s_wom"))
                    W(q.dve, SEM("s_wom"), mv)
                    q.dve.append(lambda e, par=par: e.tensor_tensor(
                        out=row[par][:], in0=pba[:, 0:2048], in1=bob_s[:], op=ALU.add))
                    rv = DI(q.dve, SEM("s_row"))
                    row_tok = (SEM("s_row"), rv)
                    W(q.dve, *t_x)
                    DR(q.dve)
                    q.dve.append(lambda e, par=par: e.scalar_tensor_tensor(
                        out=row[par][:], in0=row[par][:], scalar=1.0, in1=xo_t[par][:],
                        op0=ALU.mult, op1=ALU.add, accum_out=rsum[par][:]))
                    DR(q.dve)
                    q.dve.append(lambda e, par=par: e.tensor_scalar(
                        out=nmean[par][:], in0=rsum[par][:], scalar1=-1.0 / D, scalar2=None,
                        op0=ALU.mult))
                    lm = DI(q.dve, SEM("s_ln"))
                    W(q.act, SEM("s_ln"), lm)
                    q.act.append(lambda e, par=par: e.activation(
                        out=xo_t[par][:], in_=row[par][:], func=AF.Square, bias=nmean[par][:],
                        accum_out=varp[par][:]))
                    la = DI(q.act, SEM("s_ln"))
                    W(q.dve, SEM("s_ln"), la)
                    q.dve.append(lambda e, par=par: e.tensor_scalar(
                        out=varp[par][:], in0=varp[par][:], scalar1=1.0 / D, scalar2=1e-5,
                        op0=ALU.mult, op1=ALU.add))
                    lv2 = DI(q.dve, SEM("s_ln"))
                    W(q.act, SEM("s_ln"), lv2)
                    q.act.append(lambda e, par=par: e.activation(
                        out=stdv[par][:], in_=varp[par][:], func=AF.Sqrt))
                    la2 = DI(q.act, SEM("s_ln"))
                    W(q.dve, SEM("s_ln"), la2)
                    q.dve.append(lambda e, par=par: e.reciprocal(rstd[par][:], stdv[par][:]))
                    DR(q.dve)
                    for tk_ in st_tok[par] or []:
                        W(q.dve, *tk_)
                    q.dve.append(lambda e, par=par: e.tensor_scalar(
                        out=y32t[par][:], in0=row[par][:], scalar1=nmean[par][:],
                        scalar2=rstd[par][:], op0=ALU.add, op1=ALU.mult))
                    DR(q.dve)
                    q.dve.append(lambda e, par=par: e.tensor_copy(ybt[par][:], y32t[par][:]))
                    lnv = DI(q.dve, SEM("s_ln"))
                    ln_tok[par] = (SEM("s_ln"), lnv)
                    W(q.act, SEM("s_ln"), lnv)
                    DMA(q.act, f"s_yw{par}",
                        lambda e, tt=tt, par=par: e.dma_start(
                            out=y32[tt * 128:(tt + 1) * 128, :], in_=y32t[par][:]))
                    t_b = DMA(q.act, f"s_yw{par}",
                              lambda e, tt=tt, par=par: e.dma_start(
                                  out=yb[tt * 128:(tt + 1) * 128, :], in_=ybt[par][:]))
                    st_tok[par] = [t_b]
                yw_final = st_tok[0] + st_tok[1]
                retire_dmas(q, sems, retired)
                emit_block(nc, q, "wo")
        else:
            yw_final = []

        if "fa" in phases:
            with contextlib.ExitStack() as fctx:
                A = fctx.enter_context
                Y_fm = A(nc.sbuf_tensor("Y_fm", [128, 16 * T], BF16, is_transpose_dest=True))
                W1b = [A(nc.sbuf_tensor(f"W1b{i}", [128, 2048], BF16)) for i in range(4)]
                W3b = [A(nc.sbuf_tensor(f"W3b{i}", [128, 2048], BF16)) for i in range(4)]
                h_sb = [A(nc.sbuf_tensor(f"h_sb{i}", [128, T], BF16)) for i in range(2)]
                hh_sb = [A(nc.sbuf_tensor(f"hh_sb{i}", [128, T], BF16)) for i in range(2)]
                tmp_f = [A(nc.sbuf_tensor(f"tmp_f{i}", [128, 1024], F32)) for i in range(2)]

                q = Q()
                for tk in yw_final:
                    W(q.sp, *tk)
                if "p0" in phases:
                    W(q.pe, *t_h128)
                for j in range(16):
                    DMA(q.sp, "s_ymt",
                        lambda e, j=j: e.dma_start_transpose(
                            out=Y_fm[:, j * T:(j + 1) * T], in_=yb[:, j * 128:(j + 1) * 128]))
                W(q.pe, SEM("s_ymt"), SEM("s_ymt").n)

                w1t_v = w1b16.rearrange("(c p) f -> p c f", p=128)
                w3t_v = w3b16.rearrange("(c p) f -> p c f", p=128)
                if "att" in phases:
                    for k in range(2):
                        W(q.pool, SEM(f"s_wcw{k}"), SEM(f"s_wcw{k}").n)

                panel_tok = [None] * 4
                hfree_tok = [None, None]
                silu_tok = [None, None]
                hsb_tok = [None, None]
                hev_tok = [None, None]
                for i in range(64):
                    par = i % 2
                    wpar = i % 4
                    DMA(q.pool, f"s_w13{wpar}",
                        lambda e, i=i, wpar=wpar: e.dma_start(
                            out=W1b[wpar].ap().rearrange("p (c f) -> p c f", c=16),
                            in_=w1t_v[:, :, i * 128:(i + 1) * 128]),
                        wait_tok=panel_tok[wpar])
                    t3 = DMA(q.pool, f"s_w13{wpar}",
                             lambda e, i=i, wpar=wpar: e.dma_start(
                                 out=W3b[wpar].ap().rearrange("p (c f) -> p c f", c=16),
                                 in_=w3t_v[:, :, i * 128:(i + 1) * 128]))
                    W(q.pe, *t3)
                    if hfree_tok[par] is not None:
                        W(q.act, *hfree_tok[par])
                    pend = None
                    for tqp in range(2):
                        tqs = (2 * tqp, 2 * tqp + 1)
                        if silu_tok[0] is not None:
                            W(q.pe, *silu_tok[0])
                        if hsb_tok[0] is not None:
                            W(q.pe, *hsb_tok[0])
                        for tq in tqs:
                            for c in range(16):
                                q.pe.append(lambda e, wpar=wpar, c=c, tq=tq: e.matmul(
                                    pb[tq % 2][:], W1b[wpar][:, c * 128:(c + 1) * 128],
                                    Y_fm[:, c * T + tq * 512:c * T + (tq + 1) * 512],
                                    start=(c == 0), stop=(c == 15)))
                            for c in range(16):
                                q.pe.append(lambda e, wpar=wpar, c=c, tq=tq: e.matmul(
                                    pb[2 + tq % 2][:], W3b[wpar][:, c * 128:(c + 1) * 128],
                                    Y_fm[:, c * T + tq * 512:c * T + (tq + 1) * 512],
                                    start=(c == 0), stop=(c == 15)))
                        mv = DI(q.pe, SEM("s_ug"))
                        W(q.act, SEM("s_ug"), mv)
                        q.act.append(lambda e, tqp=tqp: e.activation(
                            out=tmp_f[tqp][:], in_=pba[:, 0:1024], func=AF.Silu))
                        sv = DI(q.act, SEM("s_silu"))
                        silu_tok = [(SEM("s_silu"), sv), (SEM("s_silu"), sv)]
                        W(q.dve, SEM("s_silu"), sv)
                        q.dve.append(lambda e, tqp=tqp, par=par: e.tensor_tensor(
                            out=h_sb[par][:, tqp * 1024:(tqp + 1) * 1024], in0=pba[:, 1024:2048],
                            in1=tmp_f[tqp][:], op=ALU.mult))
                        hv = DI(q.dve, SEM("s_hsb"))
                        hsb_tok = [(SEM("s_hsb"), hv), (SEM("s_hsb"), hv)]
                        if pend is not None:
                            ppar, ptqp, phv = pend
                            W(q.pe, SEM("s_hsb"), phv)
                            if hev_tok[0] is not None:
                                W(q.pe, *hev_tok[0])
                            for tq in (2 * ptqp, 2 * ptqp + 1):
                                q.pe.append(lambda e, ppar=ppar, tq=tq, hb=4 + tq % 2: e.matmul(
                                    pb[hb][:], h128b[:], h_sb[ppar][:, tq * 512:(tq + 1) * 512],
                                    start=True, stop=True))
                            h1v = DI(q.pe, SEM("s_h128"))
                            W(q.act, SEM("s_h128"), h1v)
                            q.act.append(lambda e, ppar=ppar, ptqp=ptqp: e.activation(
                                out=hh_sb[ppar][:, ptqp * 1024:(ptqp + 1) * 1024], in_=pba[:, 2048:3072],
                                func=AF.Copy, scale=sc8192[:]))
                            hev = DI(q.act, SEM("s_hhev"))
                            hev_tok = [(SEM("s_hhev"), hev), (SEM("s_hhev"), hev)]
                        pend = (par, tqp, hv)
                    ppar, ptqp, phv = pend
                    W(q.pe, SEM("s_hsb"), phv)
                    if hev_tok[0] is not None:
                        W(q.pe, *hev_tok[0])
                    for tq in (2 * ptqp, 2 * ptqp + 1):
                        q.pe.append(lambda e, ppar=ppar, tq=tq, hb=4 + tq % 2: e.matmul(
                            pb[hb][:], h128b[:], h_sb[ppar][:, tq * 512:(tq + 1) * 512],
                            start=True, stop=True))
                    h1v = DI(q.pe, SEM("s_h128"))
                    W(q.act, SEM("s_h128"), h1v)
                    q.act.append(lambda e, ppar=ppar, ptqp=ptqp: e.activation(
                        out=hh_sb[ppar][:, ptqp * 1024:(ptqp + 1) * 1024], in_=pba[:, 2048:3072],
                        func=AF.Copy, scale=sc8192[:]))
                    hev = DI(q.act, SEM("s_hhev"))
                    hev_tok = [(SEM("s_hhev"), hev), (SEM("s_hhev"), hev)]
                    panel_tok[wpar] = (SEM("s_ug"), SEM("s_ug").n)
                    W(q.sp, SEM("s_hhev"), SEM("s_hhev").n)
                    hfree_tok[par] = DMA(q.sp, f"s_hhw{par}",
                                         lambda e, i=i, par=par: e.dma_start(
                                             out=hh[i * 128:(i + 1) * 128, :], in_=hh_sb[par][:]))
                hh_final = [hfree_tok[0], hfree_tok[1]]
                retire_dmas(q, sems, retired)
                emit_block(nc, q, "fa")
        else:
            hh_final = [None, None]

        if "fb" in phases:
            with contextlib.ExitStack() as fctx:
                A = fctx.enter_context
                NWB = 8
                HHc = [A(nc.sbuf_tensor(f"HHc{i}", [128, 64 * 256], BF16)) for i in range(2)]
                W2pb = [A(nc.sbuf_tensor(f"W2pb{i}", [128, D], BF16)) for i in range(NWB)]
                y32t2 = [A(nc.sbuf_tensor(f"y32t2{i}", [128, D], F32)) for i in range(2)]
                row2 = [A(nc.sbuf_tensor(f"row2{i}", [128, D], F32)) for i in range(2)]
                outt = [A(nc.sbuf_tensor(f"outt{i}", [128, D], F32)) for i in range(2)]
                sqscr2 = A(nc.sbuf_tensor("sqscr2", [128, D], F32))
                rsum2 = [A(nc.sbuf_tensor(f"rsum2{i}", [128, 1], F32)) for i in range(2)]
                nmean2 = [A(nc.sbuf_tensor(f"nmean2{i}", [128, 1], F32)) for i in range(2)]
                varp2 = [A(nc.sbuf_tensor(f"varp2{i}", [128, 1], F32)) for i in range(2)]
                stdv2 = [A(nc.sbuf_tensor(f"stdv2{i}", [128, 1], F32)) for i in range(2)]
                rstd2 = [A(nc.sbuf_tensor(f"rstd2{i}", [128, 1], F32)) for i in range(2)]

                q = Q()
                hh_v = hh.rearrange("(i p) (ch t) -> p i ch t", p=128, t=256)
                if hh_final[0] is not None:
                    W(q.sp, *hh_final[0])
                    W(q.sp, *hh_final[1])
                if "p0" in phases:
                    for k in range(4):
                        W(q.sp, SEM(f"s_w2w{k}"), SEM(f"s_w2w{k}").n)
                        W(q.pool, SEM(f"s_w2w{k}"), SEM(f"s_w2w{k}").n)

                hhc_free = [None, None]
                w2_free = [None] * NWB

                def load_hhc(ch):
                    par = ch % 2
                    return DMA(q.sp, f"s_hhl{par}",
                               lambda e, ch=ch, par=par: e.dma_start(
                                   out=HHc[par].ap().rearrange("p (i t) -> p i t", i=64),
                                   in_=hh_v[:, :, ch, :]),
                               wait_tok=hhc_free[par])

                hhl = load_hhc(0)
                evac_tok = [None, None]
                y32_tok = [None, None]
                out_tok = [None, None]
                for ch in range(8):
                    par = ch % 2
                    if ch + 1 < 8:
                        hhl_next = load_hhc(ch + 1)
                    W(q.pe, *hhl)
                    for i in range(64):
                        wb = i % NWB
                        ring = q.sp if i % 2 == 0 else q.pool
                        wt = DMA(ring, f"s_w2r{wb}",
                                 lambda e, i=i, wb=wb: e.dma_start(
                                     out=W2pb[wb][:], in_=w2p[i * 128:(i + 1) * 128, :]),
                                 wait_tok=w2_free[wb])
                        W(q.pe, *wt)
                        if i == 0:
                            for t2 in range(2):
                                if evac_tok[t2] is not None:
                                    W(q.pe, *evac_tok[t2])
                        for ts2 in range(2):
                            for dc in range(4):
                                bank = ts2 * 4 + dc
                                q.pe.append(lambda e, i=i, ts2=ts2, dc=dc, bank=bank, par=par, wb=wb: e.matmul(
                                    pb[bank][:], HHc[par][:, i * 256 + ts2 * 128:i * 256 + (ts2 + 1) * 128],
                                    W2pb[wb][:, dc * 512:(dc + 1) * 512],
                                    start=(i == 0), stop=(i == 63)))
                        if i % 4 == 3:
                            fi = DI(q.pe, SEM("s_fbi"))
                            for k in range(4):
                                w2_free[(i - 3 + k) % NWB] = (SEM("s_fbi"), fi)
                    fbm = DI(q.pe, SEM("s_fbm"))
                    hhc_free[par] = (SEM("s_fbm"), fbm)
                    for ts2 in range(2):
                        tt = ch * 2 + ts2
                        lpar = tt % 2
                        if y32_tok[lpar] is not None:
                            W(q.act, *y32_tok[lpar])
                        t_y = DMA(q.act, f"s_y32l{lpar}",
                                  lambda e, tt=tt, lpar=lpar: e.dma_start(
                                      out=y32t2[lpar][:], in_=y32[tt * 128:(tt + 1) * 128, :]))
                        W(q.dve, SEM("s_fbm"), fbm)
                        q.dve.append(lambda e, ts2=ts2, lpar=lpar: e.tensor_copy(
                            row2[lpar][:], pba[:, ts2 * 2048:(ts2 + 1) * 2048]))
                        ev = DI(q.dve, SEM("s_fbe"))
                        evac_tok[ts2] = (SEM("s_fbe"), ev)
                        W(q.dve, *t_y)
                        DR(q.dve)
                        q.dve.append(lambda e, lpar=lpar: e.scalar_tensor_tensor(
                            out=row2[lpar][:], in0=row2[lpar][:], scalar=1.0, in1=y32t2[lpar][:],
                            op0=ALU.mult, op1=ALU.add, accum_out=rsum2[lpar][:]))
                        DR(q.dve)
                        q.dve.append(lambda e, lpar=lpar: e.tensor_scalar(
                            out=nmean2[lpar][:], in0=rsum2[lpar][:], scalar1=-1.0 / D, scalar2=None,
                            op0=ALU.mult))
                        lm = DI(q.dve, SEM("s_ln"))
                        W(q.act, SEM("s_ln"), lm)
                        q.act.append(lambda e, lpar=lpar: e.activation(
                            out=sqscr2[:], in_=row2[lpar][:], func=AF.Square, bias=nmean2[lpar][:],
                            accum_out=varp2[lpar][:]))
                        la = DI(q.act, SEM("s_ln"))
                        W(q.dve, SEM("s_ln"), la)
                        q.dve.append(lambda e, lpar=lpar: e.tensor_scalar(
                            out=varp2[lpar][:], in0=varp2[lpar][:], scalar1=1.0 / D, scalar2=1e-5,
                            op0=ALU.mult, op1=ALU.add))
                        lv2 = DI(q.dve, SEM("s_ln"))
                        W(q.act, SEM("s_ln"), lv2)
                        q.act.append(lambda e, lpar=lpar: e.activation(
                            out=stdv2[lpar][:], in_=varp2[lpar][:], func=AF.Sqrt))
                        la2 = DI(q.act, SEM("s_ln"))
                        W(q.dve, SEM("s_ln"), la2)
                        q.dve.append(lambda e, lpar=lpar: e.reciprocal(rstd2[lpar][:], stdv2[lpar][:]))
                        DR(q.dve)
                        if out_tok[lpar] is not None:
                            W(q.dve, *out_tok[lpar])
                        q.dve.append(lambda e, lpar=lpar: e.tensor_scalar(
                            out=outt[lpar][:], in0=row2[lpar][:], scalar1=nmean2[lpar][:],
                            scalar2=rstd2[lpar][:], op0=ALU.add, op1=ALU.mult))
                        lnv = DI(q.dve, SEM("s_ln"))
                        y32_tok[lpar] = (SEM("s_ln"), lnv)
                        W(q.act, SEM("s_ln"), lnv)
                        out_tok[lpar] = DMA(q.act, f"s_out{lpar}",
                                            lambda e, tt=tt, lpar=lpar: e.dma_start(
                                                out=out[tt * 128:(tt + 1) * 128, :], in_=outt[lpar][:]))
                    if ch + 1 < 8:
                        hhl = hhl_next
                retire_dmas(q, sems, retired)
                emit_block(nc, q, "fb")

    return nc




def _hadamard(n):
    h = np.array([[1.0]], dtype=np.float32)
    while h.shape[0] < n:
        h = np.block([[h, h], [h, -h]])
    return h.astype(np.float32)


def _host_inputs(inputs):
    x = np.asarray(inputs["L0_attn_normalized_resid_pre"], dtype=np.float32)
    WQ = np.asarray(inputs["L0_attn_W_Q"], dtype=np.float32)
    WK = np.asarray(inputs["L0_attn_W_K"], dtype=np.float32)
    WV = np.asarray(inputs["L0_attn_W_V"], dtype=np.float32)
    WO = np.asarray(inputs["L0_attn_W_O"], dtype=np.float32)
    bQ = np.asarray(inputs["L0_attn_b_Q"], dtype=np.float32)
    bK = np.asarray(inputs["L0_attn_b_K"], dtype=np.float32)
    bV = np.asarray(inputs["L0_attn_b_V"], dtype=np.float32)
    bO = np.asarray(inputs["L0_attn_b_O"], dtype=np.float32)
    W1 = np.asarray(inputs["L0_ffn_W1"], dtype=np.float32)
    W2 = np.asarray(inputs["L0_ffn_W2"], dtype=np.float32)
    W3 = np.asarray(inputs["L0_ffn_W3"], dtype=np.float32)

    shared = {
        "wq": np.ascontiguousarray(WQ),
        "wk": np.ascontiguousarray(WK),
        "wvr": np.ascontiguousarray(WV.transpose(1, 0, 2).reshape(D, NH * E)),
        "wo": np.ascontiguousarray(WO),
        "w1t": np.ascontiguousarray(W1.T),
        "w3t": np.ascontiguousarray(W3.T),
        "w2t": np.ascontiguousarray(W2.T),
        "bqt": np.ascontiguousarray(bQ.T),
        "bkt": np.ascontiguousarray(bK.T),
        "bvb": np.ascontiguousarray(np.broadcast_to(bV.reshape(1, NH * E), (128, NH * E))),
        "bob": np.ascontiguousarray(np.broadcast_to(bO.reshape(1, D), (128, D))),
        "h64": _hadamard(64),
        "h128": _hadamard(128),
    }
    in_maps = []
    for c in range(8):
        b, par = c // 2, c % 2
        pos = POS[par]
        x_own = np.ascontiguousarray(np.concatenate([x[b, p:p + 512] for p in pos], axis=0))
        q = np.zeros((NSLOT, 128, 512), np.float32)
        for s in range(NSLOT):
            q[s] = pos[s] + np.arange(512, dtype=np.float32)[None, :] - np.arange(128, dtype=np.float32)[:, None]
        m = dict(shared)
        m["x_own"] = x_own
        m["x_batch"] = np.ascontiguousarray(x[b])
        m["qmp"] = q
        in_maps.append(m)
    return in_maps


def kernel(**inputs):
    in_maps = _host_inputs(inputs)
    nc = build()
    res = run_bass_kernel_spmd(nc, in_maps, core_ids=list(range(8)))
    full = np.empty((4, S, D), np.float32)
    for c in range(8):
        b, par = c // 2, c % 2
        oc = res.results[c]["out"]
        for s, p in enumerate(POS[par]):
            full[b, p:p + 512] = oc[s * 512:(s + 1) * 512]
    return full


# version 13
# speedup vs baseline: 1.0318x; 1.0318x over previous
import contextlib
import numpy as np
import ml_dtypes

import concourse.bass as bass
import concourse.mybir as mybir
from concourse.bass_utils import run_bass_kernel_spmd

F32 = mybir.dt.float32
BF16 = mybir.dt.bfloat16
AF = mybir.ActivationFunctionType
ALU = mybir.AluOpType

D = 2048
S = 4096
T = 2048
NH = 16
E = 128
HF = 8192
NG = 8
LIM = [1024, 2048, 3072, 4096]
NSLOT = 4
NMASK = 8
POS = {0: [0, 1536, 2048, 3584], 1: [512, 1024, 2560, 3072]}
EXP_BIAS = -30.0
INV_SQRT_DH = 0.08838834764831845
INV_SQRT_HF = 0.011048543456039806


class SemT:

    def __init__(self, ctx, nc, name):
        self.h = ctx.enter_context(nc.semaphore(name))
        self.n = 0

    is_dma = False

    def dma(self):
        self.is_dma = True
        self.n += 16
        return self.n

    def bump(self):
        self.n += 1
        return self.n


class Q:

    def __init__(self):
        self.pe = []
        self.act = []
        self.dve = []
        self.pool = []
        self.sp = []

    def run_block(self, nc, name):
        with nc.Block(name):
            blk = None


def retire_dmas(q, sems, last):
    for nm, s_ in sems.items():
        if s_.is_dma and s_.n > last.get(nm, 0):
            W(q.act, s_, s_.n)
            last[nm] = s_.n


def emit_block(nc, q, name):
    with nc.Block(name) as blk:
        @blk.tensor
        def _(e):
            for f in q.pe:
                f(e)

        @blk.scalar
        def _(e):
            for f in q.act:
                f(e)

        @blk.vector
        def _(e):
            for f in q.dve:
                f(e)

        @blk.gpsimd
        def _(e):
            for f in q.pool:
                f(e)

        @blk.sync
        def _(e):
            for f in q.sp:
                f(e)


def W(lst, sem, val):
    if val > 0:
        lst.append(lambda e, h=sem.h, v=val: e.wait_ge(h, v))


def DI(lst, sem):
    v = sem.bump()
    lst.append(lambda e, h=sem.h: e.drain().then_inc(h, 1))
    return v


def DR(lst):
    lst.append(lambda e: e.drain())


def build(dump=None, phases=("p0", "att", "wo", "fa", "fb")):
    nc = bass.Bass("TRN2", target_bir_lowering=False, debug=False)

    def dt_kind(name):
        return "ExternalOutput" if dump == name else "Internal"

    x_own = nc.dram_tensor("x_own", [T, D], F32, kind="ExternalInput").ap()
    x_batch = nc.dram_tensor("x_batch", [S, D], F32, kind="ExternalInput").ap()
    wq = nc.dram_tensor("wq", [NH, D, E], F32, kind="ExternalInput").ap()
    wk = nc.dram_tensor("wk", [NH, D, E], F32, kind="ExternalInput").ap()
    wvr = nc.dram_tensor("wvr", [D, NH * E], F32, kind="ExternalInput").ap()
    wo = nc.dram_tensor("wo", [NH, E, D], F32, kind="ExternalInput").ap()
    w1t = nc.dram_tensor("w1t", [D, HF], F32, kind="ExternalInput").ap()
    w3t = nc.dram_tensor("w3t", [D, HF], F32, kind="ExternalInput").ap()
    w2t = nc.dram_tensor("w2t", [HF, D], F32, kind="ExternalInput").ap()
    bqt = nc.dram_tensor("bqt", [E, NH], F32, kind="ExternalInput").ap()
    bkt = nc.dram_tensor("bkt", [E, NH], F32, kind="ExternalInput").ap()
    bvb = nc.dram_tensor("bvb", [128, NH * E], F32, kind="ExternalInput").ap()
    bob = nc.dram_tensor("bob", [128, D], F32, kind="ExternalInput").ap()
    qmp = nc.dram_tensor("qmp", [NSLOT, 128, 512], F32, kind="ExternalInput").ap()
    h64 = nc.dram_tensor("h64", [64, 64], F32, kind="ExternalInput").ap()
    h128 = nc.dram_tensor("h128", [128, 128], F32, kind="ExternalInput").ap()
    id128 = nc.dram_tensor("id128", [128, 128], F32, kind="ExternalInput").ap()

    out = nc.dram_tensor("out", [T, D], F32, kind="ExternalOutput").ap()

    xob = nc.dram_tensor("xob", [T, D], BF16, kind=dt_kind("xob")).ap()
    xof = nc.dram_tensor("xof", [D, T], BF16, kind=dt_kind("xof")).ap()
    xbb = nc.dram_tensor("xbb", [S, D], BF16, kind=dt_kind("xbb")).ap()
    xbf = nc.dram_tensor("xbf", [D, S], BF16, kind=dt_kind("xbf")).ap()
    w2p = nc.dram_tensor("w2p", [HF, D], BF16, kind=dt_kind("w2p")).ap()
    w1b16 = nc.dram_tensor("w1b16", [D, HF], BF16, kind="Internal").ap()
    w3b16 = nc.dram_tensor("w3b16", [D, HF], BF16, kind="Internal").ap()
    zf = nc.dram_tensor("zf", [NH * E, T], BF16, kind=dt_kind("zf")).ap()
    yb = nc.dram_tensor("yb", [T, D], BF16, kind=dt_kind("yb")).ap()
    y32 = nc.dram_tensor("y32", [T, D], F32, kind=dt_kind("y32")).ap()
    hh = nc.dram_tensor("hh", [HF, T], BF16, kind=dt_kind("hh")).ap()

    with contextlib.ExitStack() as ctx:
        E_ = ctx.enter_context
        psum = E_(nc.psum_tensor("psum", [128, 4096], F32))
        pba = psum.ap()
        pb = [pba[:, i * 512:(i + 1) * 512] for i in range(8)]
        h64b = E_(nc.sbuf_tensor("h64b", [64, 64], BF16))
        h128b = E_(nc.sbuf_tensor("h128b", [128, 128], BF16))
        expb = E_(nc.sbuf_tensor("expb", [128, 1], F32))
        scl128 = E_(nc.sbuf_tensor("scl128", [128, 1], F32))
        sc8192 = E_(nc.sbuf_tensor("sc8192", [128, 1], F32))
        idb = E_(nc.sbuf_tensor("idb", [128, 128], BF16))

        sems = {}
        retired = {}

        def SEM(name):
            if name not in sems:
                sems[name] = SemT(ctx, nc, name)
            return sems[name]

        def DMA(lst, sem_name, fn, wait_tok=None):
            s = SEM(sem_name)
            if wait_tok is not None:
                W(lst, *wait_tok)
            v = s.dma()
            lst.append(lambda e, fn=fn, h=s.h: fn(e).then_inc(h, 16))
            return (s, v)

        if "p0" in phases:
            with contextlib.ExitStack() as p0ctx:
                P = p0ctx.enter_context
                xo_sb = [P(nc.sbuf_tensor(f"xo_sb{i}", [128, D], BF16)) for i in range(4)]
                w2i = [P(nc.sbuf_tensor(f"w2i{i}", [64, D], BF16)) for i in range(4)]
                w2o = [P(nc.sbuf_tensor(f"w2o{i}", [64, D], BF16)) for i in range(4)]

                q = Q()
                t_h64 = DMA(q.pool, "s_cst0", lambda e: e.dma_start(out=h64b[:], in_=h64))
                t_h128 = DMA(q.pool, "s_cst1", lambda e: e.dma_start(out=h128b[:], in_=h128))
                q.dve.append(lambda e: e.memset(expb[:], EXP_BIAS))
                q.dve.append(lambda e: e.memset(scl128[:], INV_SQRT_DH))
                q.dve.append(lambda e: e.memset(sc8192[:], INV_SQRT_HF))

                t_id = DMA(q.pool, "s_cid", lambda e: e.dma_start(out=idb[:], in_=id128))
                W(q.pe, *t_id)
                pbf = pba.bitcast(BF16)
                bank_tok = [None] * 4
                ld_tok = [None] * 4

                def xpass(srcm, dstm, ntile, slabs, sname):
                    sev0 = SEM("s_sev").n
                    for tt in range(ntile):
                        par = tt % 4
                        lt = DMA(q.pool, f"s_xoc{par}",
                                 lambda e, tt=tt, par=par, srcm=srcm: e.dma_start(
                                     out=xo_sb[par][:], in_=srcm[tt * 128:(tt + 1) * 128, :]),
                                 wait_tok=ld_tok[par])
                        W(q.pe, *lt)
                        for c in range(16):
                            bk = c % 4
                            if bank_tok[bk] is not None:
                                W(q.pe, *bank_tok[bk])
                            q.pe.append(lambda e, c=c, par=par, bk=bk: e.transpose(
                                pbf[:, bk * 1024:bk * 1024 + 128],
                                xo_sb[par][:, c * 128:(c + 1) * 128], idb[:]))
                            if c % 4 == 3:
                                mv = DI(q.pe, SEM("s_trm"))
                                W(q.dve, SEM("s_trm"), mv)
                                for cc in range(c - 3, c + 1):
                                    q.dve.append(lambda e, cc=cc, tt=tt: e.tensor_copy(
                                        slabs[cc][:, tt * 128:(tt + 1) * 128],
                                        pbf[:, (cc % 4) * 1024:(cc % 4) * 1024 + 128]))
                                ev = DI(q.dve, SEM("s_sev"))
                                for cc in range(c - 3, c + 1):
                                    bank_tok[cc % 4] = (SEM("s_sev"), ev)
                        ld_tok[par] = bank_tok[3]
                    sev_all = SEM("s_sev").n
                    toks = []
                    for c in range(16):
                        W(q.act, SEM("s_sev"), sev_all)
                        toks.append(DMA(q.act, f"s_xfw{c % 4}",
                                        lambda e, c=c, dstm=dstm, slabs=slabs: e.dma_start(
                                            out=dstm[c * 128:(c + 1) * 128, :], in_=slabs[c][:])))
                    return toks

                with contextlib.ExitStack() as s1:
                    slabs = [s1.enter_context(nc.sbuf_tensor(f"xsl{i}", [128, T], BF16)) for i in range(16)]
                    xfw_all = xpass(x_own, xof, 16, slabs, "xof")
                    for k in range(4):
                        W(q.act, SEM(f"s_xfw{k}"), SEM(f"s_xfw{k}").n)
                with contextlib.ExitStack() as s2:
                    slabs = [s2.enter_context(nc.sbuf_tensor(f"xbl{i}", [128, S], BF16)) for i in range(16)]
                    for k in range(4):
                        W(q.dve, SEM(f"s_xfw{k}"), SEM(f"s_xfw{k}").n)
                    xbfw_all = xpass(x_batch, xbf, 32, slabs, "xbf")
                    for k in range(4):
                        W(q.act, SEM(f"s_xfw{k}"), SEM(f"s_xfw{k}").n)

                w2t_v = w2t.rearrange("(j q) d -> q j d", q=128)
                w2p_v = w2p.rearrange("(j q) d -> q j d", q=128)
                W(q.pe, *t_h64)
                wr_tok = [None] * 4
                mm_tok = [None] * 4
                fbank_tok = [None] * 4
                for p in range(128):
                    fpar = p % 4
                    lt = DMA(q.pool, f"s_w2l{fpar}",
                             lambda e, p=p, fpar=fpar: e.dma_start(out=w2i[fpar][:], in_=w2t_v[p]),
                             wait_tok=mm_tok[fpar])
                    W(q.pe, *lt)
                    if wr_tok[fpar] is not None:
                        W(q.dve, *wr_tok[fpar])
                    for c in range(4):
                        if fbank_tok[c] is not None:
                            W(q.pe, *fbank_tok[c])
                    for c in range(4):
                        q.pe.append(lambda e, fpar=fpar, c=c: e.matmul(
                            pb[4 + c][0:64, :], h64b[:], w2i[fpar][:, c * 512:(c + 1) * 512],
                            start=True, stop=True))
                    mv = DI(q.pe, SEM("s_w2mm"))
                    mm_tok[fpar] = (SEM("s_w2mm"), mv)
                    W(q.dve, SEM("s_w2mm"), mv)
                    for c in range(4):
                        q.dve.append(lambda e, fpar=fpar, c=c: e.tensor_copy(
                            w2o[fpar][:, c * 512:(c + 1) * 512], pb[4 + c][0:64, :]))
                    ev = DI(q.dve, SEM("s_w2ev"))
                    for c in range(4):
                        fbank_tok[c] = (SEM("s_w2ev"), ev)
                    W(q.act, SEM("s_w2ev"), ev)
                    wr_tok[fpar] = DMA(q.act, f"s_w2w{fpar}",
                                       lambda e, p=p, fpar=fpar: e.dma_start(out=w2p_v[p], in_=w2o[fpar][:]))

                retire_dmas(q, sems, retired)
                emit_block(nc, q, "p0")
        else:
            xfw_all = xbfw_all = [None]

        if "att" in phases:
            with contextlib.ExitStack() as actx:
                A = actx.enter_context
                K2 = A(nc.sbuf_tensor("K2", [128, 2 * S], BF16))
                V2e = A(nc.sbuf_tensor("V2e", [128, 32 * 258], BF16))
                Q2 = A(nc.sbuf_tensor("Q2", [128, 2 * T], BF16))
                PT = [A(nc.sbuf_tensor("PT0", [128, 24 * 512], BF16)),
                      A(nc.sbuf_tensor("PT1", [128, 32 * 512], BF16))]
                WQb = [A(nc.sbuf_tensor(f"WQb{i}", [128, 4096], BF16)) for i in range(2)]
                WKb = [A(nc.sbuf_tensor(f"WKb{i}", [128, 4096], BF16)) for i in range(2)]
                WVb = [A(nc.sbuf_tensor(f"WVb{i}", [128, 4096], BF16)) for i in range(2)]
                xcol = [A(nc.sbuf_tensor(f"xcol{i}", [128, 8192], BF16)) for i in range(2)]
                zTst = [A(nc.sbuf_tensor(f"zTst{i}", [128, T], BF16, is_transpose_dest=True)) for i in range(2)]
                z_sb = [A(nc.sbuf_tensor(f"z_sb{i}", [128, 128], BF16)) for i in range(4)]
                rz_sb = A(nc.sbuf_tensor("rz_sb", [128, 4], F32))
                qmp_s = A(nc.sbuf_tensor("qmp_s", [128, NSLOT * 512], F32))
                bqt_s = A(nc.sbuf_tensor("bqt_s", [128, NH], F32))
                bkt_s = A(nc.sbuf_tensor("bkt_s", [128, NH], F32))
                bvb_g = [A(nc.sbuf_tensor(f"bvb_g{i}", [128, 256], F32)) for i in range(2)]

                q = Q()
                DMA(q.act, "s_catt", lambda e: e.dma_start(
                    out=qmp_s.ap().rearrange("p (s f) -> p s f", s=NSLOT),
                    in_=qmp.rearrange("s p f -> p s f")))
                DMA(q.act, "s_catt", lambda e: e.dma_start(out=bqt_s[:], in_=bqt))
                ct = [DMA(q.act, "s_catt", lambda e: e.dma_start(out=bkt_s[:], in_=bkt))]

                w13s = [A(nc.sbuf_tensor(f"w13s{i}", [128, 2048], BF16)) for i in range(2)]
                v2e_ones = V2e.ap().rearrange("p (k h e) -> p k h e", k=32, h=2)[:, :, :, 128:129]
                q.pool.append(lambda e: e.memset(v2e_ones, 1.0))
                ones_tok = (SEM("s_misc"), DI(q.pool, SEM("s_misc")))

                xof_v = xof.rearrange("(c p) t -> p c t", p=128)
                xbf_v = xbf.rearrange("(c p) t -> p c t", p=128)
                xcol_v = [xcol[i].ap().rearrange("p (c t) -> p c t", c=16) for i in range(2)]

                projdone = [None]

                bvb_free = [None, None]

                def load_group_weights(g, prev_projdone):
                    par = g % 2
                    toks = []
                    toks.append(DMA(q.pool, f"s_bv{par}",
                                    lambda e, g=g, par=par: e.dma_start(
                                        out=bvb_g[par][:], in_=bvb[:, g * 256:(g + 1) * 256]),
                                    wait_tok=bvb_free[par]))
                    wq_src = wq[2 * g:2 * g + 2].rearrange("h (c p) e -> p h c e", p=128)
                    wk_src = wk[2 * g:2 * g + 2].rearrange("h (c p) e -> p h c e", p=128)
                    wv_src = wvr[:, g * 256:(g + 1) * 256].rearrange("(c p) n -> p c n", p=128)
                    toks.append(DMA(q.pool, f"s_wq{par}", lambda e, s=wq_src, d=WQb[par]: e.dma_start(
                        out=d.ap().rearrange("p (h c e) -> p h c e", h=2, c=16), in_=s),
                        wait_tok=prev_projdone))
                    toks.append(DMA(q.pool, f"s_wk{par}", lambda e, s=wk_src, d=WKb[par]: e.dma_start(
                        out=d.ap().rearrange("p (h c e) -> p h c e", h=2, c=16), in_=s)))
                    toks.append(DMA(q.pool, f"s_wv{par}", lambda e, s=wv_src, d=WVb[par]: e.dma_start(
                        out=d.ap().rearrange("p (c n) -> p c n", c=16), in_=s)))
                    return toks

                def load_xcols(g):
                    toks = []
                    for i in range(12):
                        gi = g * 12 + i
                        bi = gi % 2
                        wait_tok = None
                        if i < 2:
                            if g >= 1:
                                wait_tok = (SEM("s_pk"), 8 * (g - 1) + (7 + i))
                        elif i < 4:
                            wait_tok = (SEM("s_pq"), 2 * g + 1)
                        elif i < 6:
                            wait_tok = (SEM("s_pq"), 2 * g + 2)
                        else:
                            wait_tok = (SEM("s_pk"), 8 * g + (i - 6) + 1)
                        if i < 4:
                            src = xof_v[:, :, i * 512:(i + 1) * 512]
                        else:
                            kc = i - 4
                            src = xbf_v[:, :, kc * 512:(kc + 1) * 512]
                        toks.append(DMA(q.pool, f"s_xl{bi}",
                                        lambda e, s=src, d=xcol_v[bi]: e.dma_start(out=d, in_=s),
                                        wait_tok=wait_tok))
                    return toks

                if "p0" in phases:
                    for k in range(4):
                        W(q.pool, SEM(f"s_xfw{k}"), SEM(f"s_xfw{k}").n)

                w13_src = [w1t.rearrange("(c p) f -> p c f", p=128), w3t.rearrange("(c p) f -> p c f", p=128)]
                w13_dst = [w1b16.rearrange("(c p) f -> p c f", p=128), w3b16.rearrange("(c p) f -> p c f", p=128)]
                w13_st = [None, None]

                def conv_w13(k):
                    wsel, r = divmod(k, 64)
                    cidx, fidx = r % 8, r // 8
                    par = k % 2
                    lt = DMA(q.pool, f"s_wcv{par}",
                             lambda e, wsel=wsel, cidx=cidx, fidx=fidx, par=par: e.dma_start(
                                 out=w13s[par].ap().rearrange("p (c f) -> p c f", c=2),
                                 in_=w13_src[wsel][:, 2 * cidx:2 * cidx + 2, 1024 * fidx:1024 * fidx + 1024]),
                             wait_tok=w13_st[par])
                    W(q.sp, *lt)
                    w13_st[par] = DMA(q.sp, f"s_wcw{par}",
                                      lambda e, wsel=wsel, cidx=cidx, fidx=fidx, par=par: e.dma_start(
                                          out=w13_dst[wsel][:, 2 * cidx:2 * cidx + 2, 1024 * fidx:1024 * fidx + 1024],
                                          in_=w13s[par].ap().rearrange("p (c f) -> p c f", c=2)))

                wg_toks = load_group_weights(0, None)
                x_toks = load_xcols(0)

                tokA = [None] * 4
                tokZ = [None] * 4
                zhead = 0
                ztr_tok = [None] * 4
                zfw_tok = [None, None]
                for t in ct:
                    W(q.pe, *t)
                    W(q.act, *t)
                    W(q.dve, *t)
                W(q.pe, *ones_tok)

                for g in range(NG):
                    par = g % 2
                    for k in range(16 * g, 16 * g + 16):
                        conv_w13(k)
                    for t in wg_toks:
                        W(q.pe, *t)

                    for tqp in range(2):
                        for tq in (2 * tqp, 2 * tqp + 1):
                            gi = g * 12 + tq
                            bi = gi % 2
                            W(q.pe, *x_toks[tq])
                            for h in range(2):
                                bank = (tq % 2) * 2 + h
                                if tokA[bank] is not None:
                                    W(q.pe, *tokA[bank])
                                for c in range(16):
                                    q.pe.append(lambda e, par=par, h=h, c=c, bi=bi, tq=tq, bank=bank: e.matmul(
                                        pb[bank][:], WQb[par][:, (h * 16 + c) * 128:(h * 16 + c + 1) * 128],
                                        xcol[bi][:, c * 512:(c + 1) * 512],
                                        start=(c == 0), stop=(c == 15)))
                        mv = DI(q.pe, SEM("s_pq"))
                        W(q.dve, SEM("s_pq"), mv)
                        for tq in (2 * tqp, 2 * tqp + 1):
                            for h in range(2):
                                hg = 2 * g + h
                                bank = (tq % 2) * 2 + h
                                q.dve.append(lambda e, h=h, tq=tq, hg=hg, bank=bank: e.tensor_scalar(
                                    out=Q2[:, h * T + tq * 512: h * T + (tq + 1) * 512],
                                    in0=pb[bank][:], scalar1=bqt_s[:, hg:hg + 1], scalar2=None,
                                    op0=ALU.add))
                        ev = DI(q.dve, SEM("s_pqf"))
                        for b in range(4):
                            tokA[b] = (SEM("s_pqf"), ev)

                    for kcp in range(4):
                        for kc in (2 * kcp, 2 * kcp + 1):
                            gi = g * 12 + 4 + kc
                            bi = gi % 2
                            W(q.pe, *x_toks[4 + kc])
                            for h in range(2):
                                bank = (kc % 2) * 2 + h
                                if tokA[bank] is not None:
                                    W(q.pe, *tokA[bank])
                                for c in range(16):
                                    q.pe.append(lambda e, par=par, h=h, c=c, bi=bi, bank=bank: e.matmul(
                                        pb[bank][:], WKb[par][:, (h * 16 + c) * 128:(h * 16 + c + 1) * 128],
                                        xcol[bi][:, c * 512:(c + 1) * 512],
                                        start=(c == 0), stop=(c == 15)))
                            for kt in range(4):
                                ktile = kc * 4 + kt
                                zbank = 4 + kt
                                if tokZ[kt] is not None:
                                    W(q.pe, *tokZ[kt])
                                for c in range(16):
                                    q.pe.append(lambda e, c=c, bi=bi, kt=kt, par=par, zbank=zbank: e.matmul(
                                        pb[zbank][:, 0:256],
                                        xcol[bi][:, c * 512 + kt * 128: c * 512 + (kt + 1) * 128],
                                        WVb[par][:, c * 256:(c + 1) * 256],
                                        start=(c == 0), stop=(c == 15)))
                            mv = DI(q.pe, SEM("s_pk"))
                            W(q.dve, SEM("s_pk"), mv)
                            for h in range(2):
                                hg = 2 * g + h
                                bank = (kc % 2) * 2 + h
                                q.dve.append(lambda e, h=h, kc=kc, hg=hg, bank=bank: e.tensor_scalar(
                                    out=K2[:, h * S + kc * 512: h * S + (kc + 1) * 512],
                                    in0=pb[bank][:], scalar1=bkt_s[:, hg:hg + 1], scalar2=None,
                                    op0=ALU.add))
                            kev = DI(q.dve, SEM("s_pkf"))
                            tokA[(kc % 2) * 2] = (SEM("s_pkf"), kev)
                            tokA[(kc % 2) * 2 + 1] = (SEM("s_pkf"), kev)
                            for kt in range(4):
                                ktile = kc * 4 + kt
                                vout = V2e.ap().rearrange("p (k h e) -> p k h e", k=32, h=2)[:, ktile, :, 0:128]
                                q.dve.append(lambda e, vout=vout, kt=kt, par=par: e.tensor_tensor(
                                    out=vout,
                                    in0=pb[4 + kt][:, 0:256].rearrange("p (h e) -> p h e", h=2),
                                    in1=bvb_g[par].ap().rearrange("p (h e) -> p h e", h=2),
                                    op=ALU.add))
                            vev = DI(q.dve, SEM("s_pvf"))
                            bvb_free[par] = (SEM("s_pvf"), vev)
                            for kt in range(4):
                                tokZ[kt] = (SEM("s_pvf"), vev)
                    projdone[0] = (SEM("s_pk"), 8 * (g + 1))

                    if g + 1 < NG:
                        wg_next = load_group_weights(g + 1, projdone[0])
                        x_next = load_xcols(g + 1)

                    pkf_target = SEM("s_pkf").n
                    pqf_target = SEM("s_pqf").n
                    pvf_target = SEM("s_pvf").n

                    for h in range(2):
                        hg = 2 * g + h
                        W(q.pe, SEM("s_pkf"), pkf_target)
                        W(q.pe, SEM("s_pqf"), pqf_target)
                        W(q.pe, SEM("s_pvf"), pvf_target)
                        if zfw_tok[hg % 2] is not None:
                            W(q.sp, *zfw_tok[hg % 2])

                        slot_state = []

                        def emit_scores(s):
                            nj = LIM[s] // 128
                            exp_base = SEM("s_exp").n
                            for j in range(nj):
                                bank = j % 4
                                if j < 4:
                                    if tokA[bank] is not None:
                                        W(q.pe, *tokA[bank])
                                else:
                                    W(q.pe, SEM("s_exp"), exp_base + j // 4)
                                q.pe.append(lambda e, h=h, j=j, s=s, bank=bank: e.matmul(
                                    pb[bank][:], K2[:, h * S + j * 128:h * S + (j + 1) * 128],
                                    Q2[:, h * T + s * 512:h * T + (s + 1) * 512],
                                    start=True, stop=True))
                                if j % 4 == 3:
                                    mv = DI(q.pe, SEM("s_sc"))
                                    W(q.act, SEM("s_sc"), mv)
                                    W(q.act, SEM("s_av"), SEM("s_av").n)
                                    jj0 = j - 3
                                    q.act.append(lambda e, jj0=jj0, sp2=s % 2: e.activation(
                                        out=PT[sp2][:, jj0 * 512:(jj0 + 4) * 512], in_=pba[:, 0:2048],
                                        func=AF.Exp, bias=expb[:], scale=scl128[:]))
                                    DI(q.act, SEM("s_exp"))
                            exp_total = SEM("s_exp").n
                            for b in range(4):
                                tokA[b] = (SEM("s_exp"), exp_total)
                            exp_base2 = exp_base
                            for j in range(nj - NMASK, nj):
                                W(q.dve, SEM("s_exp"), exp_base2 + j // 4 + 1)
                                q.dve.append(lambda e, j=j, s=s, sp2=s % 2: e.scalar_tensor_tensor(
                                    out=PT[sp2][:, j * 512:(j + 1) * 512],
                                    in0=qmp_s[:, s * 512:(s + 1) * 512], scalar=float(128 * j),
                                    in1=PT[sp2][:, j * 512:(j + 1) * 512],
                                    op0=ALU.is_ge, op1=ALU.mult))
                            mk = DI(q.dve, SEM("s_mask"))
                            return (s, exp_total, mk, nj)

                        def emit_av(state):
                            nonlocal_z = state
                            s, exp_total, mk, nj = state
                            jorder = list(range(nj - NMASK)) + list(range(nj - NMASK, nj))
                            W(q.pe, SEM("s_exp"), exp_total)
                            for q4 in range(4):
                                zb = 4 + q4
                                if tokZ[q4] is not None:
                                    W(q.pe, *tokZ[q4])
                                for ji, j in enumerate(jorder):
                                    if j == nj - NMASK or (nj == NMASK and ji == 0):
                                        W(q.pe, SEM("s_mask"), mk)
                                    q.pe.append(lambda e, j=j, q4=q4, h=h, zb=zb, ji=ji, nj=nj, sp2=s % 2: e.matmul(
                                        pb[zb][:, 0:129],
                                        PT[sp2][:, j * 512 + q4 * 128: j * 512 + (q4 + 1) * 128],
                                        V2e[:, j * 258 + h * 129: j * 258 + (h + 1) * 129],
                                        start=(ji == 0), stop=(ji == nj - 1)))
                            av = DI(q.pe, SEM("s_av"))
                            W(q.dve, SEM("s_av"), av)
                            for q4 in range(4):
                                zb = 4 + q4
                                if ztr_tok[q4] is not None:
                                    W(q.dve, *ztr_tok[q4])
                                q.dve.append(lambda e, q4=q4, zb=zb: e.reciprocal(
                                    rz_sb[:, q4:q4 + 1], pb[zb][:, 128:129]))
                                DR(q.dve)
                                q.dve.append(lambda e, q4=q4, zb=zb: e.tensor_scalar(
                                    out=z_sb[q4][:], in0=pb[zb][:, 0:128],
                                    scalar1=rz_sb[:, q4:q4 + 1], scalar2=None, op0=ALU.mult))
                                zev = DI(q.dve, SEM("s_zev"))
                                tokZ[q4] = (SEM("s_zev"), zev)
                                W(q.sp, SEM("s_zev"), zev)
                                col = (s * 4 + q4) * 128
                                ztr_tok[q4] = DMA(q.sp, f"s_ztr{q4}",
                                                  lambda e, q4=q4, col=col, h=h: e.dma_start_transpose(
                                                      out=zTst[h][:, col:col + 128], in_=z_sb[q4][:]))

                        prev = None
                        for s in range(NSLOT):
                            st = emit_scores(s)
                            if prev is not None:
                                emit_av(prev)
                            prev = st
                        emit_av(prev)

                        for tk in ztr_tok:
                            if tk is not None:
                                W(q.sp, *tk)
                        zfw_tok[hg % 2] = DMA(q.sp, f"s_zfw{hg % 2}",
                                              lambda e, hg=hg, h=h: e.dma_start(
                                                  out=zf[hg * 128:(hg + 1) * 128, :], in_=zTst[h][:]))
                    if g + 1 < NG:
                        wg_toks = wg_next
                        x_toks = x_next

                zfw_final = [zfw_tok[0], zfw_tok[1]]
                retire_dmas(q, sems, retired)
                emit_block(nc, q, "att")
        else:
            zfw_final = [None, None]

        if "wo" in phases:
            with contextlib.ExitStack() as wctx:
                A = wctx.enter_context
                z_fm = A(nc.sbuf_tensor("z_fm", [128, NH * T], BF16))
                WOb = A(nc.sbuf_tensor("WOb", [128, NH * D], BF16))
                bob_s = A(nc.sbuf_tensor("bob_s", [128, D], F32))
                xo_t = [A(nc.sbuf_tensor(f"xo_t{i}", [128, D], F32)) for i in range(2)]
                row = [A(nc.sbuf_tensor(f"row{i}", [128, D], F32)) for i in range(2)]
                y32t = [A(nc.sbuf_tensor(f"y32t{i}", [128, D], F32)) for i in range(2)]
                ybt = [A(nc.sbuf_tensor(f"ybt{i}", [128, D], BF16)) for i in range(2)]
                rsum = [A(nc.sbuf_tensor(f"rsum{i}", [128, 1], F32)) for i in range(2)]
                nmean = [A(nc.sbuf_tensor(f"nmean{i}", [128, 1], F32)) for i in range(2)]
                varp = [A(nc.sbuf_tensor(f"varp{i}", [128, 1], F32)) for i in range(2)]
                stdv = [A(nc.sbuf_tensor(f"stdv{i}", [128, 1], F32)) for i in range(2)]
                rstd = [A(nc.sbuf_tensor(f"rstd{i}", [128, 1], F32)) for i in range(2)]

                q = Q()
                if zfw_final[0] is not None:
                    W(q.sp, *zfw_final[0])
                    W(q.sp, *zfw_final[1])
                t_z = DMA(q.sp, "s_zl", lambda e: e.dma_start(
                    out=z_fm.ap().rearrange("p (h t) -> p h t", h=NH),
                    in_=zf.rearrange("(h p) t -> p h t", p=128)))
                t_wo = DMA(q.pool, "s_wol", lambda e: e.dma_start(
                    out=WOb.ap().rearrange("p (h d) -> p h d", h=NH),
                    in_=wo.rearrange("h p d -> p h d")))
                t_bo = DMA(q.act, "s_cbo", lambda e: e.dma_start(out=bob_s[:], in_=bob))
                W(q.pe, *t_z)
                W(q.pe, *t_wo)
                W(q.dve, *t_bo)

                ln_tok = [None, None]
                st_tok = [None, None]
                ybt_tok = [None]
                row_tok = None
                for tt in range(16):
                    par = tt % 2
                    if ln_tok[par] is not None:
                        W(q.act, *ln_tok[par])
                    t_x = DMA(q.act, f"s_xrl{par}",
                              lambda e, tt=tt, par=par: e.dma_start(
                                  out=xo_t[par][:], in_=x_own[tt * 128:(tt + 1) * 128, :]))
                    if row_tok is not None:
                        W(q.pe, *row_tok)
                    for dc in range(4):
                        for h in range(NH):
                            q.pe.append(lambda e, h=h, tt=tt, dc=dc: e.matmul(
                                pb[dc][:], z_fm[:, h * T + tt * 128:h * T + (tt + 1) * 128],
                                WOb[:, h * D + dc * 512:h * D + (dc + 1) * 512],
                                start=(h == 0), stop=(h == NH - 1)))
                    mv = DI(q.pe, SEM("s_wom"))
                    W(q.dve, SEM("s_wom"), mv)
                    q.dve.append(lambda e, par=par: e.tensor_tensor(
                        out=row[par][:], in0=pba[:, 0:2048], in1=bob_s[:], op=ALU.add))
                    rv = DI(q.dve, SEM("s_row"))
                    row_tok = (SEM("s_row"), rv)
                    W(q.dve, *t_x)
                    DR(q.dve)
                    q.dve.append(lambda e, par=par: e.scalar_tensor_tensor(
                        out=row[par][:], in0=row[par][:], scalar=1.0, in1=xo_t[par][:],
                        op0=ALU.mult, op1=ALU.add, accum_out=rsum[par][:]))
                    DR(q.dve)
                    q.dve.append(lambda e, par=par: e.tensor_scalar(
                        out=nmean[par][:], in0=rsum[par][:], scalar1=-1.0 / D, scalar2=None,
                        op0=ALU.mult))
                    lm = DI(q.dve, SEM("s_ln"))
                    W(q.act, SEM("s_ln"), lm)
                    q.act.append(lambda e, par=par: e.activation(
                        out=xo_t[par][:], in_=row[par][:], func=AF.Square, bias=nmean[par][:],
                        accum_out=varp[par][:]))
                    la = DI(q.act, SEM("s_ln"))
                    W(q.dve, SEM("s_ln"), la)
                    q.dve.append(lambda e, par=par: e.tensor_scalar(
                        out=varp[par][:], in0=varp[par][:], scalar1=1.0 / D, scalar2=1e-5,
                        op0=ALU.mult, op1=ALU.add))
                    lv2 = DI(q.dve, SEM("s_ln"))
                    W(q.act, SEM("s_ln"), lv2)
                    q.act.append(lambda e, par=par: e.activation(
                        out=stdv[par][:], in_=varp[par][:], func=AF.Sqrt))
                    la2 = DI(q.act, SEM("s_ln"))
                    W(q.dve, SEM("s_ln"), la2)
                    q.dve.append(lambda e, par=par: e.reciprocal(rstd[par][:], stdv[par][:]))
                    DR(q.dve)
                    for tk_ in st_tok[par] or []:
                        W(q.dve, *tk_)
                    q.dve.append(lambda e, par=par: e.tensor_scalar(
                        out=y32t[par][:], in0=row[par][:], scalar1=nmean[par][:],
                        scalar2=rstd[par][:], op0=ALU.add, op1=ALU.mult))
                    DR(q.dve)
                    q.dve.append(lambda e, par=par: e.tensor_copy(ybt[par][:], y32t[par][:]))
                    lnv = DI(q.dve, SEM("s_ln"))
                    ln_tok[par] = (SEM("s_ln"), lnv)
                    W(q.act, SEM("s_ln"), lnv)
                    DMA(q.act, f"s_yw{par}",
                        lambda e, tt=tt, par=par: e.dma_start(
                            out=y32[tt * 128:(tt + 1) * 128, :], in_=y32t[par][:]))
                    t_b = DMA(q.act, f"s_yw{par}",
                              lambda e, tt=tt, par=par: e.dma_start(
                                  out=yb[tt * 128:(tt + 1) * 128, :], in_=ybt[par][:]))
                    st_tok[par] = [t_b]
                yw_final = st_tok[0] + st_tok[1]
                retire_dmas(q, sems, retired)
                emit_block(nc, q, "wo")
        else:
            yw_final = []

        if "fa" in phases:
            with contextlib.ExitStack() as fctx:
                A = fctx.enter_context
                Y_fm = A(nc.sbuf_tensor("Y_fm", [128, 16 * T], BF16, is_transpose_dest=True))
                W1b = [A(nc.sbuf_tensor(f"W1b{i}", [128, 2048], BF16)) for i in range(4)]
                W3b = [A(nc.sbuf_tensor(f"W3b{i}", [128, 2048], BF16)) for i in range(4)]
                h_sb = [A(nc.sbuf_tensor(f"h_sb{i}", [128, T], BF16)) for i in range(2)]
                hh_sb = [A(nc.sbuf_tensor(f"hh_sb{i}", [128, T], BF16)) for i in range(2)]
                tmp_f = [A(nc.sbuf_tensor(f"tmp_f{i}", [128, 1024], F32)) for i in range(2)]

                q = Q()
                for tk in yw_final:
                    W(q.sp, *tk)
                if "p0" in phases:
                    W(q.pe, *t_h128)
                for j in range(16):
                    DMA(q.sp, "s_ymt",
                        lambda e, j=j: e.dma_start_transpose(
                            out=Y_fm[:, j * T:(j + 1) * T], in_=yb[:, j * 128:(j + 1) * 128]))
                W(q.pe, SEM("s_ymt"), SEM("s_ymt").n)

                w1t_v = w1b16.rearrange("(c p) f -> p c f", p=128)
                w3t_v = w3b16.rearrange("(c p) f -> p c f", p=128)
                if "att" in phases:
                    for k in range(2):
                        W(q.pool, SEM(f"s_wcw{k}"), SEM(f"s_wcw{k}").n)

                panel_tok = [None] * 4
                hfree_tok = [None, None]
                silu_tok = [None, None]
                hsb_tok = [None, None]
                hev_tok = [None, None]
                for i in range(64):
                    par = i % 2
                    wpar = i % 4
                    DMA(q.pool, f"s_w13{wpar}",
                        lambda e, i=i, wpar=wpar: e.dma_start(
                            out=W1b[wpar].ap().rearrange("p (c f) -> p c f", c=16),
                            in_=w1t_v[:, :, i * 128:(i + 1) * 128]),
                        wait_tok=panel_tok[wpar])
                    t3 = DMA(q.pool, f"s_w13{wpar}",
                             lambda e, i=i, wpar=wpar: e.dma_start(
                                 out=W3b[wpar].ap().rearrange("p (c f) -> p c f", c=16),
                                 in_=w3t_v[:, :, i * 128:(i + 1) * 128]))
                    W(q.pe, *t3)
                    if hfree_tok[par] is not None:
                        W(q.act, *hfree_tok[par])
                    pend = None
                    for tqp in range(2):
                        tqs = (2 * tqp, 2 * tqp + 1)
                        if silu_tok[0] is not None:
                            W(q.pe, *silu_tok[0])
                        if hsb_tok[0] is not None:
                            W(q.pe, *hsb_tok[0])
                        for tq in tqs:
                            for c in range(16):
                                q.pe.append(lambda e, wpar=wpar, c=c, tq=tq: e.matmul(
                                    pb[tq % 2][:], W1b[wpar][:, c * 128:(c + 1) * 128],
                                    Y_fm[:, c * T + tq * 512:c * T + (tq + 1) * 512],
                                    start=(c == 0), stop=(c == 15)))
                            for c in range(16):
                                q.pe.append(lambda e, wpar=wpar, c=c, tq=tq: e.matmul(
                                    pb[2 + tq % 2][:], W3b[wpar][:, c * 128:(c + 1) * 128],
                                    Y_fm[:, c * T + tq * 512:c * T + (tq + 1) * 512],
                                    start=(c == 0), stop=(c == 15)))
                        mv = DI(q.pe, SEM("s_ug"))
                        W(q.act, SEM("s_ug"), mv)
                        q.act.append(lambda e, tqp=tqp: e.activation(
                            out=tmp_f[tqp][:], in_=pba[:, 0:1024], func=AF.Silu))
                        sv = DI(q.act, SEM("s_silu"))
                        silu_tok = [(SEM("s_silu"), sv), (SEM("s_silu"), sv)]
                        W(q.dve, SEM("s_silu"), sv)
                        q.dve.append(lambda e, tqp=tqp, par=par: e.tensor_tensor(
                            out=h_sb[par][:, tqp * 1024:(tqp + 1) * 1024], in0=pba[:, 1024:2048],
                            in1=tmp_f[tqp][:], op=ALU.mult))
                        hv = DI(q.dve, SEM("s_hsb"))
                        hsb_tok = [(SEM("s_hsb"), hv), (SEM("s_hsb"), hv)]
                        if pend is not None:
                            ppar, ptqp, phv = pend
                            W(q.pe, SEM("s_hsb"), phv)
                            if hev_tok[0] is not None:
                                W(q.pe, *hev_tok[0])
                            for tq in (2 * ptqp, 2 * ptqp + 1):
                                q.pe.append(lambda e, ppar=ppar, tq=tq, hb=4 + tq % 2: e.matmul(
                                    pb[hb][:], h128b[:], h_sb[ppar][:, tq * 512:(tq + 1) * 512],
                                    start=True, stop=True))
                            h1v = DI(q.pe, SEM("s_h128"))
                            W(q.act, SEM("s_h128"), h1v)
                            q.act.append(lambda e, ppar=ppar, ptqp=ptqp: e.activation(
                                out=hh_sb[ppar][:, ptqp * 1024:(ptqp + 1) * 1024], in_=pba[:, 2048:3072],
                                func=AF.Copy, scale=sc8192[:]))
                            hev = DI(q.act, SEM("s_hhev"))
                            hev_tok = [(SEM("s_hhev"), hev), (SEM("s_hhev"), hev)]
                        pend = (par, tqp, hv)
                    ppar, ptqp, phv = pend
                    W(q.pe, SEM("s_hsb"), phv)
                    if hev_tok[0] is not None:
                        W(q.pe, *hev_tok[0])
                    for tq in (2 * ptqp, 2 * ptqp + 1):
                        q.pe.append(lambda e, ppar=ppar, tq=tq, hb=4 + tq % 2: e.matmul(
                            pb[hb][:], h128b[:], h_sb[ppar][:, tq * 512:(tq + 1) * 512],
                            start=True, stop=True))
                    h1v = DI(q.pe, SEM("s_h128"))
                    W(q.act, SEM("s_h128"), h1v)
                    q.act.append(lambda e, ppar=ppar, ptqp=ptqp: e.activation(
                        out=hh_sb[ppar][:, ptqp * 1024:(ptqp + 1) * 1024], in_=pba[:, 2048:3072],
                        func=AF.Copy, scale=sc8192[:]))
                    hev = DI(q.act, SEM("s_hhev"))
                    hev_tok = [(SEM("s_hhev"), hev), (SEM("s_hhev"), hev)]
                    panel_tok[wpar] = (SEM("s_ug"), SEM("s_ug").n)
                    W(q.sp, SEM("s_hhev"), SEM("s_hhev").n)
                    hfree_tok[par] = DMA(q.sp, f"s_hhw{par}",
                                         lambda e, i=i, par=par: e.dma_start(
                                             out=hh[i * 128:(i + 1) * 128, :], in_=hh_sb[par][:]))
                hh_final = [hfree_tok[0], hfree_tok[1]]
                retire_dmas(q, sems, retired)
                emit_block(nc, q, "fa")
        else:
            hh_final = [None, None]

        if "fb" in phases:
            with contextlib.ExitStack() as fctx:
                A = fctx.enter_context
                NWB = 8
                HHc = [A(nc.sbuf_tensor(f"HHc{i}", [128, 64 * 256], BF16)) for i in range(2)]
                W2pb = [A(nc.sbuf_tensor(f"W2pb{i}", [128, D], BF16)) for i in range(NWB)]
                y32t2 = [A(nc.sbuf_tensor(f"y32t2{i}", [128, D], F32)) for i in range(2)]
                row2 = [A(nc.sbuf_tensor(f"row2{i}", [128, D], F32)) for i in range(2)]
                outt = [A(nc.sbuf_tensor(f"outt{i}", [128, D], F32)) for i in range(2)]
                sqscr2 = A(nc.sbuf_tensor("sqscr2", [128, D], F32))
                rsum2 = [A(nc.sbuf_tensor(f"rsum2{i}", [128, 1], F32)) for i in range(2)]
                nmean2 = [A(nc.sbuf_tensor(f"nmean2{i}", [128, 1], F32)) for i in range(2)]
                varp2 = [A(nc.sbuf_tensor(f"varp2{i}", [128, 1], F32)) for i in range(2)]
                stdv2 = [A(nc.sbuf_tensor(f"stdv2{i}", [128, 1], F32)) for i in range(2)]
                rstd2 = [A(nc.sbuf_tensor(f"rstd2{i}", [128, 1], F32)) for i in range(2)]

                q = Q()
                hh_v = hh.rearrange("(i p) (ch t) -> p i ch t", p=128, t=256)
                if hh_final[0] is not None:
                    W(q.sp, *hh_final[0])
                    W(q.sp, *hh_final[1])
                if "p0" in phases:
                    for k in range(4):
                        W(q.sp, SEM(f"s_w2w{k}"), SEM(f"s_w2w{k}").n)
                        W(q.pool, SEM(f"s_w2w{k}"), SEM(f"s_w2w{k}").n)

                hhc_free = [None, None]
                w2_free = [None] * NWB

                def load_hhc(ch):
                    par = ch % 2
                    return DMA(q.sp, f"s_hhl{par}",
                               lambda e, ch=ch, par=par: e.dma_start(
                                   out=HHc[par].ap().rearrange("p (i t) -> p i t", i=64),
                                   in_=hh_v[:, :, ch, :]),
                               wait_tok=hhc_free[par])

                hhl = load_hhc(0)
                evac_tok = [None, None]
                y32_tok = [None, None]
                out_tok = [None, None]
                for ch in range(8):
                    par = ch % 2
                    if ch + 1 < 8:
                        hhl_next = load_hhc(ch + 1)
                    W(q.pe, *hhl)
                    for i in range(64):
                        wb = i % NWB
                        ring = q.sp if i % 2 == 0 else q.pool
                        wt = DMA(ring, f"s_w2r{wb}",
                                 lambda e, i=i, wb=wb: e.dma_start(
                                     out=W2pb[wb][:], in_=w2p[i * 128:(i + 1) * 128, :]),
                                 wait_tok=w2_free[wb])
                        W(q.pe, *wt)
                        if i == 0:
                            for t2 in range(2):
                                if evac_tok[t2] is not None:
                                    W(q.pe, *evac_tok[t2])
                        for ts2 in range(2):
                            for dc in range(4):
                                bank = ts2 * 4 + dc
                                q.pe.append(lambda e, i=i, ts2=ts2, dc=dc, bank=bank, par=par, wb=wb: e.matmul(
                                    pb[bank][:], HHc[par][:, i * 256 + ts2 * 128:i * 256 + (ts2 + 1) * 128],
                                    W2pb[wb][:, dc * 512:(dc + 1) * 512],
                                    start=(i == 0), stop=(i == 63)))
                        if i % 4 == 3:
                            fi = DI(q.pe, SEM("s_fbi"))
                            for k in range(4):
                                w2_free[(i - 3 + k) % NWB] = (SEM("s_fbi"), fi)
                    fbm = DI(q.pe, SEM("s_fbm"))
                    hhc_free[par] = (SEM("s_fbm"), fbm)
                    for ts2 in range(2):
                        tt = ch * 2 + ts2
                        lpar = tt % 2
                        if y32_tok[lpar] is not None:
                            W(q.act, *y32_tok[lpar])
                        t_y = DMA(q.act, f"s_y32l{lpar}",
                                  lambda e, tt=tt, lpar=lpar: e.dma_start(
                                      out=y32t2[lpar][:], in_=y32[tt * 128:(tt + 1) * 128, :]))
                        W(q.dve, SEM("s_fbm"), fbm)
                        q.dve.append(lambda e, ts2=ts2, lpar=lpar: e.tensor_copy(
                            row2[lpar][:], pba[:, ts2 * 2048:(ts2 + 1) * 2048]))
                        ev = DI(q.dve, SEM("s_fbe"))
                        evac_tok[ts2] = (SEM("s_fbe"), ev)
                        W(q.dve, *t_y)
                        DR(q.dve)
                        q.dve.append(lambda e, lpar=lpar: e.scalar_tensor_tensor(
                            out=row2[lpar][:], in0=row2[lpar][:], scalar=1.0, in1=y32t2[lpar][:],
                            op0=ALU.mult, op1=ALU.add, accum_out=rsum2[lpar][:]))
                        DR(q.dve)
                        q.dve.append(lambda e, lpar=lpar: e.tensor_scalar(
                            out=nmean2[lpar][:], in0=rsum2[lpar][:], scalar1=-1.0 / D, scalar2=None,
                            op0=ALU.mult))
                        lm = DI(q.dve, SEM("s_ln"))
                        W(q.act, SEM("s_ln"), lm)
                        q.act.append(lambda e, lpar=lpar: e.activation(
                            out=sqscr2[:], in_=row2[lpar][:], func=AF.Square, bias=nmean2[lpar][:],
                            accum_out=varp2[lpar][:]))
                        la = DI(q.act, SEM("s_ln"))
                        W(q.dve, SEM("s_ln"), la)
                        q.dve.append(lambda e, lpar=lpar: e.tensor_scalar(
                            out=varp2[lpar][:], in0=varp2[lpar][:], scalar1=1.0 / D, scalar2=1e-5,
                            op0=ALU.mult, op1=ALU.add))
                        lv2 = DI(q.dve, SEM("s_ln"))
                        W(q.act, SEM("s_ln"), lv2)
                        q.act.append(lambda e, lpar=lpar: e.activation(
                            out=stdv2[lpar][:], in_=varp2[lpar][:], func=AF.Sqrt))
                        la2 = DI(q.act, SEM("s_ln"))
                        W(q.dve, SEM("s_ln"), la2)
                        q.dve.append(lambda e, lpar=lpar: e.reciprocal(rstd2[lpar][:], stdv2[lpar][:]))
                        DR(q.dve)
                        if out_tok[lpar] is not None:
                            W(q.dve, *out_tok[lpar])
                        q.dve.append(lambda e, lpar=lpar: e.tensor_scalar(
                            out=outt[lpar][:], in0=row2[lpar][:], scalar1=nmean2[lpar][:],
                            scalar2=rstd2[lpar][:], op0=ALU.add, op1=ALU.mult))
                        lnv = DI(q.dve, SEM("s_ln"))
                        y32_tok[lpar] = (SEM("s_ln"), lnv)
                        W(q.act, SEM("s_ln"), lnv)
                        out_tok[lpar] = DMA(q.act, f"s_out{lpar}",
                                            lambda e, tt=tt, lpar=lpar: e.dma_start(
                                                out=out[tt * 128:(tt + 1) * 128, :], in_=outt[lpar][:]))
                    if ch + 1 < 8:
                        hhl = hhl_next
                retire_dmas(q, sems, retired)
                emit_block(nc, q, "fb")

    return nc




def _hadamard(n):
    h = np.array([[1.0]], dtype=np.float32)
    while h.shape[0] < n:
        h = np.block([[h, h], [h, -h]])
    return h.astype(np.float32)


def _host_inputs(inputs):
    x = np.asarray(inputs["L0_attn_normalized_resid_pre"], dtype=np.float32)
    WQ = np.asarray(inputs["L0_attn_W_Q"], dtype=np.float32)
    WK = np.asarray(inputs["L0_attn_W_K"], dtype=np.float32)
    WV = np.asarray(inputs["L0_attn_W_V"], dtype=np.float32)
    WO = np.asarray(inputs["L0_attn_W_O"], dtype=np.float32)
    bQ = np.asarray(inputs["L0_attn_b_Q"], dtype=np.float32)
    bK = np.asarray(inputs["L0_attn_b_K"], dtype=np.float32)
    bV = np.asarray(inputs["L0_attn_b_V"], dtype=np.float32)
    bO = np.asarray(inputs["L0_attn_b_O"], dtype=np.float32)
    W1 = np.asarray(inputs["L0_ffn_W1"], dtype=np.float32)
    W2 = np.asarray(inputs["L0_ffn_W2"], dtype=np.float32)
    W3 = np.asarray(inputs["L0_ffn_W3"], dtype=np.float32)

    shared = {
        "wq": np.ascontiguousarray(WQ),
        "wk": np.ascontiguousarray(WK),
        "wvr": np.ascontiguousarray(WV.transpose(1, 0, 2).reshape(D, NH * E)),
        "wo": np.ascontiguousarray(WO),
        "w1t": np.ascontiguousarray(W1.T),
        "w3t": np.ascontiguousarray(W3.T),
        "w2t": np.ascontiguousarray(W2.T),
        "bqt": np.ascontiguousarray(bQ.T),
        "bkt": np.ascontiguousarray(bK.T),
        "bvb": np.ascontiguousarray(np.broadcast_to(bV.reshape(1, NH * E), (128, NH * E))),
        "bob": np.ascontiguousarray(np.broadcast_to(bO.reshape(1, D), (128, D))),
        "h64": _hadamard(64),
        "h128": _hadamard(128),
        "id128": np.eye(128, dtype=np.float32),
    }
    in_maps = []
    for c in range(8):
        b, par = c // 2, c % 2
        pos = POS[par]
        x_own = np.ascontiguousarray(np.concatenate([x[b, p:p + 512] for p in pos], axis=0))
        q = np.zeros((NSLOT, 128, 512), np.float32)
        for s in range(NSLOT):
            q[s] = pos[s] + np.arange(512, dtype=np.float32)[None, :] - np.arange(128, dtype=np.float32)[:, None]
        m = dict(shared)
        m["x_own"] = x_own
        m["x_batch"] = np.ascontiguousarray(x[b])
        m["qmp"] = q
        in_maps.append(m)
    return in_maps


def kernel(**inputs):
    in_maps = _host_inputs(inputs)
    nc = build()
    res = run_bass_kernel_spmd(nc, in_maps, core_ids=list(range(8)))
    full = np.empty((4, S, D), np.float32)
    for c in range(8):
        b, par = c // 2, c % 2
        oc = res.results[c]["out"]
        for s, p in enumerate(POS[par]):
            full[b, p:p + 512] = oc[s * 512:(s + 1) * 512]
    return full


# version 14
# speedup vs baseline: 1.1182x; 1.0837x over previous
import contextlib
import numpy as np
import ml_dtypes

import concourse.bass as bass
import concourse.mybir as mybir
from concourse.bass_utils import run_bass_kernel_spmd

F32 = mybir.dt.float32
BF16 = mybir.dt.bfloat16
AF = mybir.ActivationFunctionType
ALU = mybir.AluOpType

D = 2048
S = 4096
T = 2048
NH = 16
E = 128
HF = 8192
NG = 8
LIM = [1024, 2048, 3072, 4096]
NSLOT = 4
NMASK = 8
POS = {0: [0, 1536, 2048, 3584], 1: [512, 1024, 2560, 3072]}
EXP_BIAS = -30.0
INV_SQRT_DH = 0.08838834764831845
INV_SQRT_HF = 0.011048543456039806


class SemT:

    def __init__(self, ctx, nc, name):
        self.h = ctx.enter_context(nc.semaphore(name))
        self.n = 0

    is_dma = False

    def dma(self):
        self.is_dma = True
        self.n += 16
        return self.n

    def bump(self):
        self.n += 1
        return self.n


class Q:

    def __init__(self):
        self.pe = []
        self.act = []
        self.dve = []
        self.pool = []
        self.sp = []

    def run_block(self, nc, name):
        with nc.Block(name):
            blk = None


def retire_dmas(q, sems, last):
    for nm, s_ in sems.items():
        if s_.is_dma and s_.n > last.get(nm, 0):
            W(q.act, s_, s_.n)
            last[nm] = s_.n


def emit_block(nc, q, name):
    with nc.Block(name) as blk:
        @blk.tensor
        def _(e):
            for f in q.pe:
                f(e)

        @blk.scalar
        def _(e):
            for f in q.act:
                f(e)

        @blk.vector
        def _(e):
            for f in q.dve:
                f(e)

        @blk.gpsimd
        def _(e):
            for f in q.pool:
                f(e)

        @blk.sync
        def _(e):
            for f in q.sp:
                f(e)


def W(lst, sem, val):
    if val > 0:
        lst.append(lambda e, h=sem.h, v=val: e.wait_ge(h, v))


def DI(lst, sem):
    v = sem.bump()
    lst.append(lambda e, h=sem.h: e.drain().then_inc(h, 1))
    return v


def DR(lst):
    lst.append(lambda e: e.drain())


def build(dump=None, phases=("p0", "att", "wo", "fa", "fb")):
    nc = bass.Bass("TRN2", target_bir_lowering=False, debug=False)

    def dt_kind(name):
        return "ExternalOutput" if dump == name else "Internal"

    x_own = nc.dram_tensor("x_own", [T, D], F32, kind="ExternalInput").ap()
    x_batch = nc.dram_tensor("x_batch", [S, D], F32, kind="ExternalInput").ap()
    wq = nc.dram_tensor("wq", [NH, D, E], F32, kind="ExternalInput").ap()
    wk = nc.dram_tensor("wk", [NH, D, E], F32, kind="ExternalInput").ap()
    wvr = nc.dram_tensor("wvr", [D, NH * E], F32, kind="ExternalInput").ap()
    wo = nc.dram_tensor("wo", [NH, E, D], F32, kind="ExternalInput").ap()
    w1t = nc.dram_tensor("w1t", [D, HF], F32, kind="ExternalInput").ap()
    w3t = nc.dram_tensor("w3t", [D, HF], F32, kind="ExternalInput").ap()
    w2t = nc.dram_tensor("w2t", [HF, D], F32, kind="ExternalInput").ap()
    bqt = nc.dram_tensor("bqt", [E, NH], F32, kind="ExternalInput").ap()
    bkt = nc.dram_tensor("bkt", [E, NH], F32, kind="ExternalInput").ap()
    bvb = nc.dram_tensor("bvb", [128, NH * E], F32, kind="ExternalInput").ap()
    bob = nc.dram_tensor("bob", [128, D], F32, kind="ExternalInput").ap()
    qmp = nc.dram_tensor("qmp", [NSLOT, 128, 512], F32, kind="ExternalInput").ap()
    h64 = nc.dram_tensor("h64", [64, 64], F32, kind="ExternalInput").ap()
    h128 = nc.dram_tensor("h128", [128, 128], F32, kind="ExternalInput").ap()
    id128 = nc.dram_tensor("id128", [128, 128], F32, kind="ExternalInput").ap()

    out = nc.dram_tensor("out", [T, D], F32, kind="ExternalOutput").ap()

    xob = nc.dram_tensor("xob", [T, D], BF16, kind=dt_kind("xob")).ap()
    xof = nc.dram_tensor("xof", [D, T], BF16, kind=dt_kind("xof")).ap()
    xbb = nc.dram_tensor("xbb", [S, D], BF16, kind=dt_kind("xbb")).ap()
    xbf = nc.dram_tensor("xbf", [D, S], BF16, kind=dt_kind("xbf")).ap()
    w2p = nc.dram_tensor("w2p", [HF, D], BF16, kind=dt_kind("w2p")).ap()
    w1b16 = nc.dram_tensor("w1b16", [D, HF], BF16, kind="Internal").ap()
    w3b16 = nc.dram_tensor("w3b16", [D, HF], BF16, kind="Internal").ap()
    zf = nc.dram_tensor("zf", [NH * E, T], BF16, kind=dt_kind("zf")).ap()
    yb = nc.dram_tensor("yb", [T, D], BF16, kind=dt_kind("yb")).ap()
    y32 = nc.dram_tensor("y32", [T, D], F32, kind=dt_kind("y32")).ap()
    hh = nc.dram_tensor("hh", [HF, T], BF16, kind=dt_kind("hh")).ap()

    with contextlib.ExitStack() as ctx:
        E_ = ctx.enter_context
        psum = E_(nc.psum_tensor("psum", [128, 4096], F32))
        pba = psum.ap()
        pb = [pba[:, i * 512:(i + 1) * 512] for i in range(8)]
        h64b = E_(nc.sbuf_tensor("h64b", [64, 64], BF16))
        h128b = E_(nc.sbuf_tensor("h128b", [128, 128], BF16))
        expb = E_(nc.sbuf_tensor("expb", [128, 1], F32))
        scl128 = E_(nc.sbuf_tensor("scl128", [128, 1], F32))
        sc8192 = E_(nc.sbuf_tensor("sc8192", [128, 1], F32))
        idb = E_(nc.sbuf_tensor("idb", [128, 128], BF16))

        sems = {}
        retired = {}

        def SEM(name):
            if name not in sems:
                sems[name] = SemT(ctx, nc, name)
            return sems[name]

        def DMA(lst, sem_name, fn, wait_tok=None):
            s = SEM(sem_name)
            if wait_tok is not None:
                W(lst, *wait_tok)
            v = s.dma()
            lst.append(lambda e, fn=fn, h=s.h: fn(e).then_inc(h, 16))
            return (s, v)

        if "p0" in phases:
            with contextlib.ExitStack() as p0ctx:
                P = p0ctx.enter_context
                xo_sb = [P(nc.sbuf_tensor(f"xo_sb{i}", [128, D], BF16)) for i in range(4)]
                w2i = [P(nc.sbuf_tensor(f"w2i{i}", [64, D], BF16)) for i in range(4)]
                w2o = [P(nc.sbuf_tensor(f"w2o{i}", [64, D], BF16)) for i in range(4)]

                q = Q()
                t_h64 = DMA(q.pool, "s_cst0", lambda e: e.dma_start(out=h64b[:], in_=h64))
                t_h128 = DMA(q.pool, "s_cst1", lambda e: e.dma_start(out=h128b[:], in_=h128))
                q.dve.append(lambda e: e.memset(expb[:], EXP_BIAS))
                q.dve.append(lambda e: e.memset(scl128[:], INV_SQRT_DH))
                q.dve.append(lambda e: e.memset(sc8192[:], INV_SQRT_HF))

                t_id = DMA(q.pool, "s_cid", lambda e: e.dma_start(out=idb[:], in_=id128))
                W(q.pe, *t_id)
                pbf = pba.bitcast(BF16)
                bank_tok = [None] * 4
                ld_tok = [None] * 4

                def xpass(srcm, dstm, ntile, slabs, sname):
                    sev0 = SEM("s_sev").n
                    for tt in range(ntile):
                        par = tt % 4
                        lt = DMA(q.pool, f"s_xoc{par}",
                                 lambda e, tt=tt, par=par, srcm=srcm: e.dma_start(
                                     out=xo_sb[par][:], in_=srcm[tt * 128:(tt + 1) * 128, :]),
                                 wait_tok=ld_tok[par])
                        W(q.pe, *lt)
                        for c in range(16):
                            bk = c % 4
                            if bank_tok[bk] is not None:
                                W(q.pe, *bank_tok[bk])
                            q.pe.append(lambda e, c=c, par=par, bk=bk: e.transpose(
                                pbf[:, bk * 1024:bk * 1024 + 128],
                                xo_sb[par][:, c * 128:(c + 1) * 128], idb[:]))
                            if c % 4 == 3:
                                mv = DI(q.pe, SEM("s_trm"))
                                W(q.dve, SEM("s_trm"), mv)
                                for cc in range(c - 3, c + 1):
                                    q.dve.append(lambda e, cc=cc, tt=tt: e.tensor_copy(
                                        slabs[cc][:, tt * 128:(tt + 1) * 128],
                                        pbf[:, (cc % 4) * 1024:(cc % 4) * 1024 + 128]))
                                ev = DI(q.dve, SEM("s_sev"))
                                for cc in range(c - 3, c + 1):
                                    bank_tok[cc % 4] = (SEM("s_sev"), ev)
                        ld_tok[par] = bank_tok[3]
                    sev_all = SEM("s_sev").n
                    toks = []
                    for c in range(16):
                        W(q.act, SEM("s_sev"), sev_all)
                        toks.append(DMA(q.act, f"s_xfw{c % 4}",
                                        lambda e, c=c, dstm=dstm, slabs=slabs: e.dma_start(
                                            out=dstm[c * 128:(c + 1) * 128, :], in_=slabs[c][:])))
                    return toks

                with contextlib.ExitStack() as s1:
                    slabs = [s1.enter_context(nc.sbuf_tensor(f"xsl{i}", [128, T], BF16)) for i in range(16)]
                    xfw_all = xpass(x_own, xof, 16, slabs, "xof")
                    for k in range(4):
                        W(q.act, SEM(f"s_xfw{k}"), SEM(f"s_xfw{k}").n)
                with contextlib.ExitStack() as s2:
                    slabs = [s2.enter_context(nc.sbuf_tensor(f"xbl{i}", [128, S], BF16)) for i in range(16)]
                    for k in range(4):
                        W(q.dve, SEM(f"s_xfw{k}"), SEM(f"s_xfw{k}").n)
                    xbfw_all = xpass(x_batch, xbf, 32, slabs, "xbf")
                    for k in range(4):
                        W(q.act, SEM(f"s_xfw{k}"), SEM(f"s_xfw{k}").n)

                w2t_v = w2t.rearrange("(j q) d -> q j d", q=128)
                w2p_v = w2p.rearrange("(j q) d -> q j d", q=128)
                W(q.pe, *t_h64)
                wr_tok = [None] * 4
                mm_tok = [None] * 4
                fbank_tok = [None] * 4
                for p in range(128):
                    fpar = p % 4
                    lt = DMA(q.pool, f"s_w2l{fpar}",
                             lambda e, p=p, fpar=fpar: e.dma_start(out=w2i[fpar][:], in_=w2t_v[p]),
                             wait_tok=mm_tok[fpar])
                    W(q.pe, *lt)
                    if wr_tok[fpar] is not None:
                        W(q.dve, *wr_tok[fpar])
                    for c in range(4):
                        if fbank_tok[c] is not None:
                            W(q.pe, *fbank_tok[c])
                    for c in range(4):
                        q.pe.append(lambda e, fpar=fpar, c=c: e.matmul(
                            pb[4 + c][0:64, :], h64b[:], w2i[fpar][:, c * 512:(c + 1) * 512],
                            start=True, stop=True))
                    mv = DI(q.pe, SEM("s_w2mm"))
                    mm_tok[fpar] = (SEM("s_w2mm"), mv)
                    W(q.dve, SEM("s_w2mm"), mv)
                    for c in range(4):
                        q.dve.append(lambda e, fpar=fpar, c=c: e.tensor_copy(
                            w2o[fpar][:, c * 512:(c + 1) * 512], pb[4 + c][0:64, :]))
                    ev = DI(q.dve, SEM("s_w2ev"))
                    for c in range(4):
                        fbank_tok[c] = (SEM("s_w2ev"), ev)
                    W(q.act, SEM("s_w2ev"), ev)
                    wr_tok[fpar] = DMA(q.act, f"s_w2w{fpar}",
                                       lambda e, p=p, fpar=fpar: e.dma_start(out=w2p_v[p], in_=w2o[fpar][:]))

                retire_dmas(q, sems, retired)
                emit_block(nc, q, "p0")
        else:
            xfw_all = xbfw_all = [None]

        if "att" in phases:
            with contextlib.ExitStack() as actx:
                A = actx.enter_context
                K2 = A(nc.sbuf_tensor("K2", [128, 2 * S], BF16))
                V2e = A(nc.sbuf_tensor("V2e", [128, 32 * 258], BF16))
                Q2 = A(nc.sbuf_tensor("Q2", [128, 2 * T], BF16))
                PT = [A(nc.sbuf_tensor("PT0", [128, 24 * 512], BF16)),
                      A(nc.sbuf_tensor("PT1", [128, 32 * 512], BF16))]
                WQb = [A(nc.sbuf_tensor(f"WQb{i}", [128, 4096], BF16)) for i in range(2)]
                WKb = [A(nc.sbuf_tensor(f"WKb{i}", [128, 4096], BF16)) for i in range(2)]
                WVb = [A(nc.sbuf_tensor(f"WVb{i}", [128, 4096], BF16)) for i in range(2)]
                xcol = [A(nc.sbuf_tensor(f"xcol{i}", [128, 8192], BF16)) for i in range(2)]
                zTst = [A(nc.sbuf_tensor(f"zTst{i}", [128, T], BF16, is_transpose_dest=True)) for i in range(2)]
                z_sb = [A(nc.sbuf_tensor(f"z_sb{i}", [128, 128], BF16)) for i in range(4)]
                rz_sb = A(nc.sbuf_tensor("rz_sb", [128, 4], F32))
                qmp_s = A(nc.sbuf_tensor("qmp_s", [128, NSLOT * 512], F32))
                bqt_s = A(nc.sbuf_tensor("bqt_s", [128, NH], F32))
                bkt_s = A(nc.sbuf_tensor("bkt_s", [128, NH], F32))
                bvb_g = [A(nc.sbuf_tensor(f"bvb_g{i}", [128, 256], F32)) for i in range(2)]

                q = Q()
                DMA(q.act, "s_catt", lambda e: e.dma_start(
                    out=qmp_s.ap().rearrange("p (s f) -> p s f", s=NSLOT),
                    in_=qmp.rearrange("s p f -> p s f")))
                DMA(q.act, "s_catt", lambda e: e.dma_start(out=bqt_s[:], in_=bqt))
                ct = [DMA(q.act, "s_catt", lambda e: e.dma_start(out=bkt_s[:], in_=bkt))]

                w13s = [A(nc.sbuf_tensor(f"w13s{i}", [128, 2048], BF16)) for i in range(2)]
                v2e_ones = V2e.ap().rearrange("p (k h e) -> p k h e", k=32, h=2)[:, :, :, 128:129]
                q.pool.append(lambda e: e.memset(v2e_ones, 1.0))
                ones_tok = (SEM("s_misc"), DI(q.pool, SEM("s_misc")))

                xof_v = xof.rearrange("(c p) t -> p c t", p=128)
                xbf_v = xbf.rearrange("(c p) t -> p c t", p=128)
                xcol_v = [xcol[i].ap().rearrange("p (c t) -> p c t", c=16) for i in range(2)]

                projdone = [None]

                bvb_free = [None, None]

                def load_group_weights(g, prev_projdone):
                    par = g % 2
                    toks = []
                    toks.append(DMA(q.pool, f"s_bv{par}",
                                    lambda e, g=g, par=par: e.dma_start(
                                        out=bvb_g[par][:], in_=bvb[:, g * 256:(g + 1) * 256]),
                                    wait_tok=bvb_free[par]))
                    wq_src = wq[2 * g:2 * g + 2].rearrange("h (c p) e -> p h c e", p=128)
                    wk_src = wk[2 * g:2 * g + 2].rearrange("h (c p) e -> p h c e", p=128)
                    wv_src = wvr[:, g * 256:(g + 1) * 256].rearrange("(c p) n -> p c n", p=128)
                    toks.append(DMA(q.pool, f"s_wq{par}", lambda e, s=wq_src, d=WQb[par]: e.dma_start(
                        out=d.ap().rearrange("p (h c e) -> p h c e", h=2, c=16), in_=s),
                        wait_tok=prev_projdone))
                    toks.append(DMA(q.pool, f"s_wk{par}", lambda e, s=wk_src, d=WKb[par]: e.dma_start(
                        out=d.ap().rearrange("p (h c e) -> p h c e", h=2, c=16), in_=s)))
                    toks.append(DMA(q.pool, f"s_wv{par}", lambda e, s=wv_src, d=WVb[par]: e.dma_start(
                        out=d.ap().rearrange("p (c n) -> p c n", c=16), in_=s)))
                    return toks

                def load_xcols(g):
                    toks = []
                    for i in range(12):
                        gi = g * 12 + i
                        bi = gi % 2
                        wait_tok = None
                        if i < 2:
                            if g >= 1:
                                wait_tok = (SEM("s_pk"), 8 * (g - 1) + (7 + i))
                        elif i < 4:
                            wait_tok = (SEM("s_pq"), 2 * g + 1)
                        elif i < 6:
                            wait_tok = (SEM("s_pq"), 2 * g + 2)
                        else:
                            wait_tok = (SEM("s_pk"), 8 * g + (i - 6) + 1)
                        if i < 4:
                            src = xof_v[:, :, i * 512:(i + 1) * 512]
                        else:
                            kc = i - 4
                            src = xbf_v[:, :, kc * 512:(kc + 1) * 512]
                        toks.append(DMA(q.pool, f"s_xl{bi}",
                                        lambda e, s=src, d=xcol_v[bi]: e.dma_start(out=d, in_=s),
                                        wait_tok=wait_tok))
                    return toks

                if "p0" in phases:
                    for k in range(4):
                        W(q.pool, SEM(f"s_xfw{k}"), SEM(f"s_xfw{k}").n)

                w13_src = [w1t.rearrange("(c p) f -> p c f", p=128), w3t.rearrange("(c p) f -> p c f", p=128)]
                w13_dst = [w1b16.rearrange("(c p) f -> p c f", p=128), w3b16.rearrange("(c p) f -> p c f", p=128)]
                w13_st = [None, None]

                def conv_w13(k):
                    wsel, r = divmod(k, 64)
                    cidx, fidx = r % 8, r // 8
                    par = k % 2
                    lt = DMA(q.pool, f"s_wcv{par}",
                             lambda e, wsel=wsel, cidx=cidx, fidx=fidx, par=par: e.dma_start(
                                 out=w13s[par].ap().rearrange("p (c f) -> p c f", c=2),
                                 in_=w13_src[wsel][:, 2 * cidx:2 * cidx + 2, 1024 * fidx:1024 * fidx + 1024]),
                             wait_tok=w13_st[par])
                    W(q.pool, *lt)
                    w13_st[par] = DMA(q.pool, f"s_wcw{par}",
                                      lambda e, wsel=wsel, cidx=cidx, fidx=fidx, par=par: e.dma_start(
                                          out=w13_dst[wsel][:, 2 * cidx:2 * cidx + 2, 1024 * fidx:1024 * fidx + 1024],
                                          in_=w13s[par].ap().rearrange("p (c f) -> p c f", c=2)))

                wg_toks = load_group_weights(0, None)
                x_toks = load_xcols(0)

                tokA = [None] * 4
                tokZ = [None] * 4
                zhead = 0
                ztr_tok = [None] * 4
                zfw_tok = [None, None]
                for t in ct:
                    W(q.pe, *t)
                    W(q.act, *t)
                    W(q.dve, *t)
                W(q.pe, *ones_tok)

                for g in range(NG):
                    par = g % 2
                    for t in wg_toks:
                        W(q.pe, *t)

                    for tqp in range(2):
                        for tq in (2 * tqp, 2 * tqp + 1):
                            gi = g * 12 + tq
                            bi = gi % 2
                            W(q.pe, *x_toks[tq])
                            for h in range(2):
                                bank = (tq % 2) * 2 + h
                                if tokA[bank] is not None:
                                    W(q.pe, *tokA[bank])
                                for c in range(16):
                                    q.pe.append(lambda e, par=par, h=h, c=c, bi=bi, tq=tq, bank=bank: e.matmul(
                                        pb[bank][:], WQb[par][:, (h * 16 + c) * 128:(h * 16 + c + 1) * 128],
                                        xcol[bi][:, c * 512:(c + 1) * 512],
                                        start=(c == 0), stop=(c == 15)))
                        mv = DI(q.pe, SEM("s_pq"))
                        W(q.dve, SEM("s_pq"), mv)
                        for tq in (2 * tqp, 2 * tqp + 1):
                            for h in range(2):
                                hg = 2 * g + h
                                bank = (tq % 2) * 2 + h
                                q.dve.append(lambda e, h=h, tq=tq, hg=hg, bank=bank: e.tensor_scalar(
                                    out=Q2[:, h * T + tq * 512: h * T + (tq + 1) * 512],
                                    in0=pb[bank][:], scalar1=bqt_s[:, hg:hg + 1], scalar2=None,
                                    op0=ALU.add))
                        ev = DI(q.dve, SEM("s_pqf"))
                        for b in range(4):
                            tokA[b] = (SEM("s_pqf"), ev)

                    for kcp in range(4):
                        for kc in (2 * kcp, 2 * kcp + 1):
                            gi = g * 12 + 4 + kc
                            bi = gi % 2
                            W(q.pe, *x_toks[4 + kc])
                            for h in range(2):
                                bank = (kc % 2) * 2 + h
                                if tokA[bank] is not None:
                                    W(q.pe, *tokA[bank])
                                for c in range(16):
                                    q.pe.append(lambda e, par=par, h=h, c=c, bi=bi, bank=bank: e.matmul(
                                        pb[bank][:], WKb[par][:, (h * 16 + c) * 128:(h * 16 + c + 1) * 128],
                                        xcol[bi][:, c * 512:(c + 1) * 512],
                                        start=(c == 0), stop=(c == 15)))
                            for kt in range(4):
                                ktile = kc * 4 + kt
                                zbank = 4 + kt
                                if tokZ[kt] is not None:
                                    W(q.pe, *tokZ[kt])
                                for c in range(16):
                                    q.pe.append(lambda e, c=c, bi=bi, kt=kt, par=par, zbank=zbank: e.matmul(
                                        pb[zbank][:, 0:256],
                                        xcol[bi][:, c * 512 + kt * 128: c * 512 + (kt + 1) * 128],
                                        WVb[par][:, c * 256:(c + 1) * 256],
                                        start=(c == 0), stop=(c == 15)))
                            mv = DI(q.pe, SEM("s_pk"))
                            W(q.dve, SEM("s_pk"), mv)
                            for h in range(2):
                                hg = 2 * g + h
                                bank = (kc % 2) * 2 + h
                                q.dve.append(lambda e, h=h, kc=kc, hg=hg, bank=bank: e.tensor_scalar(
                                    out=K2[:, h * S + kc * 512: h * S + (kc + 1) * 512],
                                    in0=pb[bank][:], scalar1=bkt_s[:, hg:hg + 1], scalar2=None,
                                    op0=ALU.add))
                            kev = DI(q.dve, SEM("s_pkf"))
                            tokA[(kc % 2) * 2] = (SEM("s_pkf"), kev)
                            tokA[(kc % 2) * 2 + 1] = (SEM("s_pkf"), kev)
                            for kt in range(4):
                                ktile = kc * 4 + kt
                                vout = V2e.ap().rearrange("p (k h e) -> p k h e", k=32, h=2)[:, ktile, :, 0:128]
                                q.dve.append(lambda e, vout=vout, kt=kt, par=par: e.tensor_tensor(
                                    out=vout,
                                    in0=pb[4 + kt][:, 0:256].rearrange("p (h e) -> p h e", h=2),
                                    in1=bvb_g[par].ap().rearrange("p (h e) -> p h e", h=2),
                                    op=ALU.add))
                            vev = DI(q.dve, SEM("s_pvf"))
                            bvb_free[par] = (SEM("s_pvf"), vev)
                            for kt in range(4):
                                tokZ[kt] = (SEM("s_pvf"), vev)
                    projdone[0] = (SEM("s_pk"), 8 * (g + 1))

                    if g + 1 < NG:
                        wg_next = load_group_weights(g + 1, projdone[0])
                        x_next = load_xcols(g + 1)
                    for k in range(16 * g, 16 * g + 16):
                        conv_w13(k)

                    pkf_target = SEM("s_pkf").n
                    pqf_target = SEM("s_pqf").n
                    pvf_target = SEM("s_pvf").n

                    for h in range(2):
                        hg = 2 * g + h
                        W(q.pe, SEM("s_pkf"), pkf_target)
                        W(q.pe, SEM("s_pqf"), pqf_target)
                        W(q.pe, SEM("s_pvf"), pvf_target)
                        if zfw_tok[hg % 2] is not None:
                            W(q.sp, *zfw_tok[hg % 2])

                        slot_state = []

                        def emit_scores(s):
                            nj = LIM[s] // 128
                            exp_base = SEM("s_exp").n
                            for j in range(nj):
                                bank = j % 4
                                if j < 4:
                                    if tokA[bank] is not None:
                                        W(q.pe, *tokA[bank])
                                else:
                                    W(q.pe, SEM("s_exp"), exp_base + j // 4)
                                q.pe.append(lambda e, h=h, j=j, s=s, bank=bank: e.matmul(
                                    pb[bank][:], K2[:, h * S + j * 128:h * S + (j + 1) * 128],
                                    Q2[:, h * T + s * 512:h * T + (s + 1) * 512],
                                    start=True, stop=True))
                                if j % 4 == 3:
                                    mv = DI(q.pe, SEM("s_sc"))
                                    W(q.act, SEM("s_sc"), mv)
                                    W(q.act, SEM("s_av"), SEM("s_av").n)
                                    jj0 = j - 3
                                    q.act.append(lambda e, jj0=jj0, sp2=s % 2: e.activation(
                                        out=PT[sp2][:, jj0 * 512:(jj0 + 4) * 512], in_=pba[:, 0:2048],
                                        func=AF.Exp, bias=expb[:], scale=scl128[:]))
                                    DI(q.act, SEM("s_exp"))
                            exp_total = SEM("s_exp").n
                            for b in range(4):
                                tokA[b] = (SEM("s_exp"), exp_total)
                            exp_base2 = exp_base
                            for j in range(nj - NMASK, nj):
                                W(q.dve, SEM("s_exp"), exp_base2 + j // 4 + 1)
                                q.dve.append(lambda e, j=j, s=s, sp2=s % 2: e.scalar_tensor_tensor(
                                    out=PT[sp2][:, j * 512:(j + 1) * 512],
                                    in0=qmp_s[:, s * 512:(s + 1) * 512], scalar=float(128 * j),
                                    in1=PT[sp2][:, j * 512:(j + 1) * 512],
                                    op0=ALU.is_ge, op1=ALU.mult))
                            mk = DI(q.dve, SEM("s_mask"))
                            return (s, exp_total, mk, nj)

                        def emit_av(state):
                            nonlocal_z = state
                            s, exp_total, mk, nj = state
                            jorder = list(range(nj - NMASK)) + list(range(nj - NMASK, nj))
                            W(q.pe, SEM("s_exp"), exp_total)
                            for q4 in range(4):
                                zb = 4 + q4
                                if tokZ[q4] is not None:
                                    W(q.pe, *tokZ[q4])
                                for ji, j in enumerate(jorder):
                                    if j == nj - NMASK or (nj == NMASK and ji == 0):
                                        W(q.pe, SEM("s_mask"), mk)
                                    q.pe.append(lambda e, j=j, q4=q4, h=h, zb=zb, ji=ji, nj=nj, sp2=s % 2: e.matmul(
                                        pb[zb][:, 0:129],
                                        PT[sp2][:, j * 512 + q4 * 128: j * 512 + (q4 + 1) * 128],
                                        V2e[:, j * 258 + h * 129: j * 258 + (h + 1) * 129],
                                        start=(ji == 0), stop=(ji == nj - 1)))
                            av = DI(q.pe, SEM("s_av"))
                            W(q.dve, SEM("s_av"), av)
                            for q4 in range(4):
                                zb = 4 + q4
                                if ztr_tok[q4] is not None:
                                    W(q.dve, *ztr_tok[q4])
                                q.dve.append(lambda e, q4=q4, zb=zb: e.reciprocal(
                                    rz_sb[:, q4:q4 + 1], pb[zb][:, 128:129]))
                                DR(q.dve)
                                q.dve.append(lambda e, q4=q4, zb=zb: e.tensor_scalar(
                                    out=z_sb[q4][:], in0=pb[zb][:, 0:128],
                                    scalar1=rz_sb[:, q4:q4 + 1], scalar2=None, op0=ALU.mult))
                                zev = DI(q.dve, SEM("s_zev"))
                                tokZ[q4] = (SEM("s_zev"), zev)
                                W(q.sp, SEM("s_zev"), zev)
                                col = (s * 4 + q4) * 128
                                ztr_tok[q4] = DMA(q.sp, f"s_ztr{q4}",
                                                  lambda e, q4=q4, col=col, h=h: e.dma_start_transpose(
                                                      out=zTst[h][:, col:col + 128], in_=z_sb[q4][:]))

                        prev = None
                        for s in range(NSLOT):
                            st = emit_scores(s)
                            if prev is not None:
                                emit_av(prev)
                            prev = st
                        emit_av(prev)

                        for tk in ztr_tok:
                            if tk is not None:
                                W(q.sp, *tk)
                        zfw_tok[hg % 2] = DMA(q.sp, f"s_zfw{hg % 2}",
                                              lambda e, hg=hg, h=h: e.dma_start(
                                                  out=zf[hg * 128:(hg + 1) * 128, :], in_=zTst[h][:]))
                    if g + 1 < NG:
                        wg_toks = wg_next
                        x_toks = x_next

                zfw_final = [zfw_tok[0], zfw_tok[1]]
                retire_dmas(q, sems, retired)
                emit_block(nc, q, "att")
        else:
            zfw_final = [None, None]

        if "wo" in phases:
            with contextlib.ExitStack() as wctx:
                A = wctx.enter_context
                z_fm = A(nc.sbuf_tensor("z_fm", [128, NH * T], BF16))
                WOb = A(nc.sbuf_tensor("WOb", [128, NH * D], BF16))
                bob_s = A(nc.sbuf_tensor("bob_s", [128, D], F32))
                xo_t = [A(nc.sbuf_tensor(f"xo_t{i}", [128, D], F32)) for i in range(2)]
                row = [A(nc.sbuf_tensor(f"row{i}", [128, D], F32)) for i in range(2)]
                y32t = [A(nc.sbuf_tensor(f"y32t{i}", [128, D], F32)) for i in range(2)]
                ybt = [A(nc.sbuf_tensor(f"ybt{i}", [128, D], BF16)) for i in range(2)]
                rsum = [A(nc.sbuf_tensor(f"rsum{i}", [128, 1], F32)) for i in range(2)]
                nmean = [A(nc.sbuf_tensor(f"nmean{i}", [128, 1], F32)) for i in range(2)]
                varp = [A(nc.sbuf_tensor(f"varp{i}", [128, 1], F32)) for i in range(2)]
                stdv = [A(nc.sbuf_tensor(f"stdv{i}", [128, 1], F32)) for i in range(2)]
                rstd = [A(nc.sbuf_tensor(f"rstd{i}", [128, 1], F32)) for i in range(2)]

                q = Q()
                if zfw_final[0] is not None:
                    W(q.sp, *zfw_final[0])
                    W(q.sp, *zfw_final[1])
                t_z = DMA(q.sp, "s_zl", lambda e: e.dma_start(
                    out=z_fm.ap().rearrange("p (h t) -> p h t", h=NH),
                    in_=zf.rearrange("(h p) t -> p h t", p=128)))
                t_wo = DMA(q.pool, "s_wol", lambda e: e.dma_start(
                    out=WOb.ap().rearrange("p (h d) -> p h d", h=NH),
                    in_=wo.rearrange("h p d -> p h d")))
                t_bo = DMA(q.act, "s_cbo", lambda e: e.dma_start(out=bob_s[:], in_=bob))
                W(q.pe, *t_z)
                W(q.pe, *t_wo)
                W(q.dve, *t_bo)

                ln_tok = [None, None]
                st_tok = [None, None]
                ybt_tok = [None]
                row_tok = None
                for tt in range(16):
                    par = tt % 2
                    if ln_tok[par] is not None:
                        W(q.act, *ln_tok[par])
                    t_x = DMA(q.act, f"s_xrl{par}",
                              lambda e, tt=tt, par=par: e.dma_start(
                                  out=xo_t[par][:], in_=x_own[tt * 128:(tt + 1) * 128, :]))
                    if row_tok is not None:
                        W(q.pe, *row_tok)
                    for dc in range(4):
                        for h in range(NH):
                            q.pe.append(lambda e, h=h, tt=tt, dc=dc: e.matmul(
                                pb[dc][:], z_fm[:, h * T + tt * 128:h * T + (tt + 1) * 128],
                                WOb[:, h * D + dc * 512:h * D + (dc + 1) * 512],
                                start=(h == 0), stop=(h == NH - 1)))
                    mv = DI(q.pe, SEM("s_wom"))
                    W(q.dve, SEM("s_wom"), mv)
                    q.dve.append(lambda e, par=par: e.tensor_tensor(
                        out=row[par][:], in0=pba[:, 0:2048], in1=bob_s[:], op=ALU.add))
                    rv = DI(q.dve, SEM("s_row"))
                    row_tok = (SEM("s_row"), rv)
                    W(q.dve, *t_x)
                    DR(q.dve)
                    q.dve.append(lambda e, par=par: e.scalar_tensor_tensor(
                        out=row[par][:], in0=row[par][:], scalar=1.0, in1=xo_t[par][:],
                        op0=ALU.mult, op1=ALU.add, accum_out=rsum[par][:]))
                    DR(q.dve)
                    q.dve.append(lambda e, par=par: e.tensor_scalar(
                        out=nmean[par][:], in0=rsum[par][:], scalar1=-1.0 / D, scalar2=None,
                        op0=ALU.mult))
                    lm = DI(q.dve, SEM("s_ln"))
                    W(q.act, SEM("s_ln"), lm)
                    q.act.append(lambda e, par=par: e.activation(
                        out=xo_t[par][:], in_=row[par][:], func=AF.Square, bias=nmean[par][:],
                        accum_out=varp[par][:]))
                    la = DI(q.act, SEM("s_ln"))
                    W(q.dve, SEM("s_ln"), la)
                    q.dve.append(lambda e, par=par: e.tensor_scalar(
                        out=varp[par][:], in0=varp[par][:], scalar1=1.0 / D, scalar2=1e-5,
                        op0=ALU.mult, op1=ALU.add))
                    lv2 = DI(q.dve, SEM("s_ln"))
                    W(q.act, SEM("s_ln"), lv2)
                    q.act.append(lambda e, par=par: e.activation(
                        out=stdv[par][:], in_=varp[par][:], func=AF.Sqrt))
                    la2 = DI(q.act, SEM("s_ln"))
                    W(q.dve, SEM("s_ln"), la2)
                    q.dve.append(lambda e, par=par: e.reciprocal(rstd[par][:], stdv[par][:]))
                    DR(q.dve)
                    for tk_ in st_tok[par] or []:
                        W(q.dve, *tk_)
                    q.dve.append(lambda e, par=par: e.tensor_scalar(
                        out=y32t[par][:], in0=row[par][:], scalar1=nmean[par][:],
                        scalar2=rstd[par][:], op0=ALU.add, op1=ALU.mult))
                    DR(q.dve)
                    q.dve.append(lambda e, par=par: e.tensor_copy(ybt[par][:], y32t[par][:]))
                    lnv = DI(q.dve, SEM("s_ln"))
                    ln_tok[par] = (SEM("s_ln"), lnv)
                    W(q.act, SEM("s_ln"), lnv)
                    DMA(q.act, f"s_yw{par}",
                        lambda e, tt=tt, par=par: e.dma_start(
                            out=y32[tt * 128:(tt + 1) * 128, :], in_=y32t[par][:]))
                    t_b = DMA(q.act, f"s_yw{par}",
                              lambda e, tt=tt, par=par: e.dma_start(
                                  out=yb[tt * 128:(tt + 1) * 128, :], in_=ybt[par][:]))
                    st_tok[par] = [t_b]
                yw_final = st_tok[0] + st_tok[1]
                retire_dmas(q, sems, retired)
                emit_block(nc, q, "wo")
        else:
            yw_final = []

        if "fa" in phases:
            with contextlib.ExitStack() as fctx:
                A = fctx.enter_context
                Y_fm = A(nc.sbuf_tensor("Y_fm", [128, 16 * T], BF16, is_transpose_dest=True))
                W1b = [A(nc.sbuf_tensor(f"W1b{i}", [128, 2048], BF16)) for i in range(4)]
                W3b = [A(nc.sbuf_tensor(f"W3b{i}", [128, 2048], BF16)) for i in range(4)]
                h_sb = [A(nc.sbuf_tensor(f"h_sb{i}", [128, T], BF16)) for i in range(2)]
                hh_sb = [A(nc.sbuf_tensor(f"hh_sb{i}", [128, T], BF16)) for i in range(2)]
                tmp_f = [A(nc.sbuf_tensor(f"tmp_f{i}", [128, 1024], F32)) for i in range(2)]

                q = Q()
                for tk in yw_final:
                    W(q.sp, *tk)
                if "p0" in phases:
                    W(q.pe, *t_h128)
                for j in range(16):
                    DMA(q.sp, "s_ymt",
                        lambda e, j=j: e.dma_start_transpose(
                            out=Y_fm[:, j * T:(j + 1) * T], in_=yb[:, j * 128:(j + 1) * 128]))
                W(q.pe, SEM("s_ymt"), SEM("s_ymt").n)

                w1t_v = w1b16.rearrange("(c p) f -> p c f", p=128)
                w3t_v = w3b16.rearrange("(c p) f -> p c f", p=128)
                if "att" in phases:
                    for k in range(2):
                        W(q.pool, SEM(f"s_wcw{k}"), SEM(f"s_wcw{k}").n)

                panel_tok = [None] * 4
                hfree_tok = [None, None]
                silu_tok = [None, None]
                hsb_tok = [None, None]
                hev_tok = [None, None]
                for i in range(64):
                    par = i % 2
                    wpar = i % 4
                    DMA(q.pool, f"s_w13{wpar}",
                        lambda e, i=i, wpar=wpar: e.dma_start(
                            out=W1b[wpar].ap().rearrange("p (c f) -> p c f", c=16),
                            in_=w1t_v[:, :, i * 128:(i + 1) * 128]),
                        wait_tok=panel_tok[wpar])
                    t3 = DMA(q.pool, f"s_w13{wpar}",
                             lambda e, i=i, wpar=wpar: e.dma_start(
                                 out=W3b[wpar].ap().rearrange("p (c f) -> p c f", c=16),
                                 in_=w3t_v[:, :, i * 128:(i + 1) * 128]))
                    W(q.pe, *t3)
                    if hfree_tok[par] is not None:
                        W(q.act, *hfree_tok[par])
                    pend = None
                    for tqp in range(2):
                        tqs = (2 * tqp, 2 * tqp + 1)
                        if silu_tok[0] is not None:
                            W(q.pe, *silu_tok[0])
                        if hsb_tok[0] is not None:
                            W(q.pe, *hsb_tok[0])
                        for tq in tqs:
                            for c in range(16):
                                q.pe.append(lambda e, wpar=wpar, c=c, tq=tq: e.matmul(
                                    pb[tq % 2][:], W1b[wpar][:, c * 128:(c + 1) * 128],
                                    Y_fm[:, c * T + tq * 512:c * T + (tq + 1) * 512],
                                    start=(c == 0), stop=(c == 15)))
                            for c in range(16):
                                q.pe.append(lambda e, wpar=wpar, c=c, tq=tq: e.matmul(
                                    pb[2 + tq % 2][:], W3b[wpar][:, c * 128:(c + 1) * 128],
                                    Y_fm[:, c * T + tq * 512:c * T + (tq + 1) * 512],
                                    start=(c == 0), stop=(c == 15)))
                        mv = DI(q.pe, SEM("s_ug"))
                        W(q.act, SEM("s_ug"), mv)
                        q.act.append(lambda e, tqp=tqp: e.activation(
                            out=tmp_f[tqp][:], in_=pba[:, 0:1024], func=AF.Silu))
                        sv = DI(q.act, SEM("s_silu"))
                        silu_tok = [(SEM("s_silu"), sv), (SEM("s_silu"), sv)]
                        W(q.dve, SEM("s_silu"), sv)
                        q.dve.append(lambda e, tqp=tqp, par=par: e.tensor_tensor(
                            out=h_sb[par][:, tqp * 1024:(tqp + 1) * 1024], in0=pba[:, 1024:2048],
                            in1=tmp_f[tqp][:], op=ALU.mult))
                        hv = DI(q.dve, SEM("s_hsb"))
                        hsb_tok = [(SEM("s_hsb"), hv), (SEM("s_hsb"), hv)]
                        if pend is not None:
                            ppar, ptqp, phv = pend
                            W(q.pe, SEM("s_hsb"), phv)
                            if hev_tok[0] is not None:
                                W(q.pe, *hev_tok[0])
                            for tq in (2 * ptqp, 2 * ptqp + 1):
                                q.pe.append(lambda e, ppar=ppar, tq=tq, hb=4 + tq % 2: e.matmul(
                                    pb[hb][:], h128b[:], h_sb[ppar][:, tq * 512:(tq + 1) * 512],
                                    start=True, stop=True))
                            h1v = DI(q.pe, SEM("s_h128"))
                            W(q.act, SEM("s_h128"), h1v)
                            q.act.append(lambda e, ppar=ppar, ptqp=ptqp: e.activation(
                                out=hh_sb[ppar][:, ptqp * 1024:(ptqp + 1) * 1024], in_=pba[:, 2048:3072],
                                func=AF.Copy, scale=sc8192[:]))
                            hev = DI(q.act, SEM("s_hhev"))
                            hev_tok = [(SEM("s_hhev"), hev), (SEM("s_hhev"), hev)]
                        pend = (par, tqp, hv)
                    ppar, ptqp, phv = pend
                    W(q.pe, SEM("s_hsb"), phv)
                    if hev_tok[0] is not None:
                        W(q.pe, *hev_tok[0])
                    for tq in (2 * ptqp, 2 * ptqp + 1):
                        q.pe.append(lambda e, ppar=ppar, tq=tq, hb=4 + tq % 2: e.matmul(
                            pb[hb][:], h128b[:], h_sb[ppar][:, tq * 512:(tq + 1) * 512],
                            start=True, stop=True))
                    h1v = DI(q.pe, SEM("s_h128"))
                    W(q.act, SEM("s_h128"), h1v)
                    q.act.append(lambda e, ppar=ppar, ptqp=ptqp: e.activation(
                        out=hh_sb[ppar][:, ptqp * 1024:(ptqp + 1) * 1024], in_=pba[:, 2048:3072],
                        func=AF.Copy, scale=sc8192[:]))
                    hev = DI(q.act, SEM("s_hhev"))
                    hev_tok = [(SEM("s_hhev"), hev), (SEM("s_hhev"), hev)]
                    panel_tok[wpar] = (SEM("s_ug"), SEM("s_ug").n)
                    W(q.sp, SEM("s_hhev"), SEM("s_hhev").n)
                    hfree_tok[par] = DMA(q.sp, f"s_hhw{par}",
                                         lambda e, i=i, par=par: e.dma_start(
                                             out=hh[i * 128:(i + 1) * 128, :], in_=hh_sb[par][:]))
                hh_final = [hfree_tok[0], hfree_tok[1]]
                retire_dmas(q, sems, retired)
                emit_block(nc, q, "fa")
        else:
            hh_final = [None, None]

        if "fb" in phases:
            with contextlib.ExitStack() as fctx:
                A = fctx.enter_context
                NWB = 8
                HHc = [A(nc.sbuf_tensor(f"HHc{i}", [128, 64 * 256], BF16)) for i in range(2)]
                W2pb = [A(nc.sbuf_tensor(f"W2pb{i}", [128, D], BF16)) for i in range(NWB)]
                y32t2 = [A(nc.sbuf_tensor(f"y32t2{i}", [128, D], F32)) for i in range(2)]
                row2 = [A(nc.sbuf_tensor(f"row2{i}", [128, D], F32)) for i in range(2)]
                outt = [A(nc.sbuf_tensor(f"outt{i}", [128, D], F32)) for i in range(2)]
                sqscr2 = A(nc.sbuf_tensor("sqscr2", [128, D], F32))
                rsum2 = [A(nc.sbuf_tensor(f"rsum2{i}", [128, 1], F32)) for i in range(2)]
                nmean2 = [A(nc.sbuf_tensor(f"nmean2{i}", [128, 1], F32)) for i in range(2)]
                varp2 = [A(nc.sbuf_tensor(f"varp2{i}", [128, 1], F32)) for i in range(2)]
                stdv2 = [A(nc.sbuf_tensor(f"stdv2{i}", [128, 1], F32)) for i in range(2)]
                rstd2 = [A(nc.sbuf_tensor(f"rstd2{i}", [128, 1], F32)) for i in range(2)]

                q = Q()
                hh_v = hh.rearrange("(i p) (ch t) -> p i ch t", p=128, t=256)
                if hh_final[0] is not None:
                    W(q.sp, *hh_final[0])
                    W(q.sp, *hh_final[1])
                if "p0" in phases:
                    for k in range(4):
                        W(q.sp, SEM(f"s_w2w{k}"), SEM(f"s_w2w{k}").n)
                        W(q.pool, SEM(f"s_w2w{k}"), SEM(f"s_w2w{k}").n)

                hhc_free = [None, None]
                w2_free = [None] * NWB

                def load_hhc(ch):
                    par = ch % 2
                    return DMA(q.sp, f"s_hhl{par}",
                               lambda e, ch=ch, par=par: e.dma_start(
                                   out=HHc[par].ap().rearrange("p (i t) -> p i t", i=64),
                                   in_=hh_v[:, :, ch, :]),
                               wait_tok=hhc_free[par])

                hhl = load_hhc(0)
                evac_tok = [None, None]
                y32_tok = [None, None]
                out_tok = [None, None]
                for ch in range(8):
                    par = ch % 2
                    if ch + 1 < 8:
                        hhl_next = load_hhc(ch + 1)
                    W(q.pe, *hhl)
                    for i in range(64):
                        wb = i % NWB
                        ring = q.sp if i % 2 == 0 else q.pool
                        wt = DMA(ring, f"s_w2r{wb}",
                                 lambda e, i=i, wb=wb: e.dma_start(
                                     out=W2pb[wb][:], in_=w2p[i * 128:(i + 1) * 128, :]),
                                 wait_tok=w2_free[wb])
                        W(q.pe, *wt)
                        if i == 0:
                            for t2 in range(2):
                                if evac_tok[t2] is not None:
                                    W(q.pe, *evac_tok[t2])
                        for ts2 in range(2):
                            for dc in range(4):
                                bank = ts2 * 4 + dc
                                q.pe.append(lambda e, i=i, ts2=ts2, dc=dc, bank=bank, par=par, wb=wb: e.matmul(
                                    pb[bank][:], HHc[par][:, i * 256 + ts2 * 128:i * 256 + (ts2 + 1) * 128],
                                    W2pb[wb][:, dc * 512:(dc + 1) * 512],
                                    start=(i == 0), stop=(i == 63)))
                        if i % 4 == 3:
                            fi = DI(q.pe, SEM("s_fbi"))
                            for k in range(4):
                                w2_free[(i - 3 + k) % NWB] = (SEM("s_fbi"), fi)
                    fbm = DI(q.pe, SEM("s_fbm"))
                    hhc_free[par] = (SEM("s_fbm"), fbm)
                    for ts2 in range(2):
                        tt = ch * 2 + ts2
                        lpar = tt % 2
                        if y32_tok[lpar] is not None:
                            W(q.act, *y32_tok[lpar])
                        t_y = DMA(q.act, f"s_y32l{lpar}",
                                  lambda e, tt=tt, lpar=lpar: e.dma_start(
                                      out=y32t2[lpar][:], in_=y32[tt * 128:(tt + 1) * 128, :]))
                        W(q.dve, SEM("s_fbm"), fbm)
                        q.dve.append(lambda e, ts2=ts2, lpar=lpar: e.tensor_copy(
                            row2[lpar][:], pba[:, ts2 * 2048:(ts2 + 1) * 2048]))
                        ev = DI(q.dve, SEM("s_fbe"))
                        evac_tok[ts2] = (SEM("s_fbe"), ev)
                        W(q.dve, *t_y)
                        DR(q.dve)
                        q.dve.append(lambda e, lpar=lpar: e.scalar_tensor_tensor(
                            out=row2[lpar][:], in0=row2[lpar][:], scalar=1.0, in1=y32t2[lpar][:],
                            op0=ALU.mult, op1=ALU.add, accum_out=rsum2[lpar][:]))
                        DR(q.dve)
                        q.dve.append(lambda e, lpar=lpar: e.tensor_scalar(
                            out=nmean2[lpar][:], in0=rsum2[lpar][:], scalar1=-1.0 / D, scalar2=None,
                            op0=ALU.mult))
                        lm = DI(q.dve, SEM("s_ln"))
                        W(q.act, SEM("s_ln"), lm)
                        q.act.append(lambda e, lpar=lpar: e.activation(
                            out=sqscr2[:], in_=row2[lpar][:], func=AF.Square, bias=nmean2[lpar][:],
                            accum_out=varp2[lpar][:]))
                        la = DI(q.act, SEM("s_ln"))
                        W(q.dve, SEM("s_ln"), la)
                        q.dve.append(lambda e, lpar=lpar: e.tensor_scalar(
                            out=varp2[lpar][:], in0=varp2[lpar][:], scalar1=1.0 / D, scalar2=1e-5,
                            op0=ALU.mult, op1=ALU.add))
                        lv2 = DI(q.dve, SEM("s_ln"))
                        W(q.act, SEM("s_ln"), lv2)
                        q.act.append(lambda e, lpar=lpar: e.activation(
                            out=stdv2[lpar][:], in_=varp2[lpar][:], func=AF.Sqrt))
                        la2 = DI(q.act, SEM("s_ln"))
                        W(q.dve, SEM("s_ln"), la2)
                        q.dve.append(lambda e, lpar=lpar: e.reciprocal(rstd2[lpar][:], stdv2[lpar][:]))
                        DR(q.dve)
                        if out_tok[lpar] is not None:
                            W(q.dve, *out_tok[lpar])
                        q.dve.append(lambda e, lpar=lpar: e.tensor_scalar(
                            out=outt[lpar][:], in0=row2[lpar][:], scalar1=nmean2[lpar][:],
                            scalar2=rstd2[lpar][:], op0=ALU.add, op1=ALU.mult))
                        lnv = DI(q.dve, SEM("s_ln"))
                        y32_tok[lpar] = (SEM("s_ln"), lnv)
                        W(q.act, SEM("s_ln"), lnv)
                        out_tok[lpar] = DMA(q.act, f"s_out{lpar}",
                                            lambda e, tt=tt, lpar=lpar: e.dma_start(
                                                out=out[tt * 128:(tt + 1) * 128, :], in_=outt[lpar][:]))
                    if ch + 1 < 8:
                        hhl = hhl_next
                retire_dmas(q, sems, retired)
                emit_block(nc, q, "fb")

    return nc




def _hadamard(n):
    h = np.array([[1.0]], dtype=np.float32)
    while h.shape[0] < n:
        h = np.block([[h, h], [h, -h]])
    return h.astype(np.float32)


def _host_inputs(inputs):
    x = np.asarray(inputs["L0_attn_normalized_resid_pre"], dtype=np.float32)
    WQ = np.asarray(inputs["L0_attn_W_Q"], dtype=np.float32)
    WK = np.asarray(inputs["L0_attn_W_K"], dtype=np.float32)
    WV = np.asarray(inputs["L0_attn_W_V"], dtype=np.float32)
    WO = np.asarray(inputs["L0_attn_W_O"], dtype=np.float32)
    bQ = np.asarray(inputs["L0_attn_b_Q"], dtype=np.float32)
    bK = np.asarray(inputs["L0_attn_b_K"], dtype=np.float32)
    bV = np.asarray(inputs["L0_attn_b_V"], dtype=np.float32)
    bO = np.asarray(inputs["L0_attn_b_O"], dtype=np.float32)
    W1 = np.asarray(inputs["L0_ffn_W1"], dtype=np.float32)
    W2 = np.asarray(inputs["L0_ffn_W2"], dtype=np.float32)
    W3 = np.asarray(inputs["L0_ffn_W3"], dtype=np.float32)

    shared = {
        "wq": np.ascontiguousarray(WQ),
        "wk": np.ascontiguousarray(WK),
        "wvr": np.ascontiguousarray(WV.transpose(1, 0, 2).reshape(D, NH * E)),
        "wo": np.ascontiguousarray(WO),
        "w1t": np.ascontiguousarray(W1.T),
        "w3t": np.ascontiguousarray(W3.T),
        "w2t": np.ascontiguousarray(W2.T),
        "bqt": np.ascontiguousarray(bQ.T),
        "bkt": np.ascontiguousarray(bK.T),
        "bvb": np.ascontiguousarray(np.broadcast_to(bV.reshape(1, NH * E), (128, NH * E))),
        "bob": np.ascontiguousarray(np.broadcast_to(bO.reshape(1, D), (128, D))),
        "h64": _hadamard(64),
        "h128": _hadamard(128),
        "id128": np.eye(128, dtype=np.float32),
    }
    in_maps = []
    for c in range(8):
        b, par = c // 2, c % 2
        pos = POS[par]
        x_own = np.ascontiguousarray(np.concatenate([x[b, p:p + 512] for p in pos], axis=0))
        q = np.zeros((NSLOT, 128, 512), np.float32)
        for s in range(NSLOT):
            q[s] = pos[s] + np.arange(512, dtype=np.float32)[None, :] - np.arange(128, dtype=np.float32)[:, None]
        m = dict(shared)
        m["x_own"] = x_own
        m["x_batch"] = np.ascontiguousarray(x[b])
        m["qmp"] = q
        in_maps.append(m)
    return in_maps


def kernel(**inputs):
    in_maps = _host_inputs(inputs)
    nc = build()
    res = run_bass_kernel_spmd(nc, in_maps, core_ids=list(range(8)))
    full = np.empty((4, S, D), np.float32)
    for c in range(8):
        b, par = c // 2, c % 2
        oc = res.results[c]["out"]
        for s, p in enumerate(POS[par]):
            full[b, p:p + 512] = oc[s * 512:(s + 1) * 512]
    return full
